# version 03
# baseline (speedup 1.0000x reference)
import numpy as np
import ml_dtypes
import concourse.bass as bass
import concourse.mybir as mybir
from concourse.bass import IndirectOffsetOnAxis
from concourse.bass_utils import run_bass_kernel_spmd
from contextlib import ExitStack

F32 = mybir.dt.float32
BF16 = mybir.dt.bfloat16
I32 = mybir.dt.int32
U32 = mybir.dt.uint32
AF = mybir.ActivationFunctionType
ALU = mybir.AluOpType
AX = mybir.AxisListType

L = 2048
DM = 1024
DI = 2048
NH = 32
HP = 64
QC = 128
NCH = L // QC
FIP = 4416
NJOBS = 4
NEXP = 8
CAP = 2048
NTIL = 24
MAGIC = 1234567

DEBUG_TAPS = []
EMIT_JOBS = NJOBS
EMIT_FFN = True
STOP_AFTER = None
RDV_LOCAL = False




def _spackap(tensor_ap, c, h_lo):
    return bass.AP(tensor_ap.tensor, c * 64 * 2048 + h_lo * 64,
                   [[2048, 64], [128, 16], [1, 64]])


def _sb(nc, name, shape, dtype):
    return nc.sbuf_tensor(name, shape, dtype, allow_name_mangling=True)


def _ps(nc, name, shape, dtype):
    return nc.psum_tensor(name, shape, dtype, allow_name_mangling=True)


def sview(base_ap, *dims):
    part = list(base_ap.ap[0])
    return bass.AP(base_ap.tensor, base_ap.offset,
                   [part] + [[int(s), int(c)] for s, c in dims])


class Prog:
    def __init__(self):
        self.steps = []

    def build(self, blk):
        for name in ("sync", "tensor", "vector", "scalar", "gpsimd"):
            steps = [fn for en, fn in self.steps if en == name]
            if not steps:
                continue

            def body(eng, steps=steps):
                for fn in steps:
                    fn(eng)
            getattr(blk, name)(body)


class K:
    pass


def _mk():
    k = K()
    nc = bass.Bass("TRN2", target_bir_lowering=False,
                   detect_race_conditions=False)
    k.nc = nc
    P = Prog()
    k.P = P
    k.in_names = []

    def d(name, shape, dtype=F32):
        k.in_names.append(name)
        return nc.dram_tensor(name, list(shape), dtype,
                              kind="ExternalInput").ap()

    k.u_jobs = d("u_jobs", [NJOBS * L, DM])
    k.wt_in = d("wt_in", [DM, FIP])
    k.wt_out = d("wt_out", [DI, DM])
    k.fcdt = d("fcdt", [DI, NH])
    k.conv_wc = d("conv_wc", [128, 18 * 7])
    k.conv_bc = d("conv_bc", [128, 18])
    k.dtb2 = d("dtb2", [64, 1])
    k.alog2 = d("alog2", [64, 1])
    k.dparr = d("dparr", [1, NH])
    k.nwc = d("nwc", [128, 16])
    k.wr = d("wr", [DM, NEXP])
    k.br = d("br", [1, NEXP])
    k.w1f = d("w1f", [NEXP * DM, DM])
    k.w2f = d("w2f", [NEXP * DM, DM])
    k.b1r = d("b1r", [NEXP, DM])
    k.b2r = d("b2r", [NEXP, DM])
    k.identb = d("identb", [128, 128], BF16)
    k.identf = d("identf", [128, 128])
    k.maskf = d("maskf", [128, 128], BF16)
    k.maskb = d("maskb", [128, 128], BF16)
    k.trili = d("trili", [128, 128], BF16)
    k.sel3 = d("sel3", [96, 4096], BF16)
    k.sel2h4 = d("sel2h4", [4, 128], BF16)
    k.iota8r = d("iota8r", [128, NEXP])
    k.iotak = d("iotak", [128, 8])
    k.tilest = d("tilest", [NEXP, NTIL])
    k.onescb = d("onescb", [128, 1], BF16)
    k.onesrb = d("onesrb", [1, 128], BF16)
    k.ones2b = d("ones2b", [2, 128], BF16)
    k.onesrf = d("onesrf", [1, 512])
    k.prefm = d("prefm", [NEXP, 1])
    k.validm = d("validm", [NEXP, NEXP], I32)
    k.vrows = d("vrows", [NEXP, 1], I32)
    k.jobrows = d("jobrows", [NJOBS, 1], I32)
    k.parrow = d("parrow", [2, 1], I32)
    k.cvals = d("cvals", [128, 8])

    k.x2_out = nc.dram_tensor("x2", [L, DM], F32, kind="ExternalOutput").ap()
    k.dbg = {}
    for name, shape, dt in DEBUG_TAPS:
        k.dbg[name] = nc.dram_tensor("dbg_" + name, list(shape), dt,
                                     kind="ExternalOutput").ap()

    def scr(name, shape, dtype):
        return nc.dram_tensor(name, list(shape), dtype, kind="Internal").ap()

    k.wtin_bf = scr("wtin_bf", [DM, FIP], BF16)
    k.w1bf = scr("w1bf", [NEXP * DM, DM], BF16)
    k.w2bf = scr("w2bf", [NEXP * DM, DM], BF16)
    k.b1hi = scr("b1hi", [NEXP, DM], BF16)
    k.b1lo = scr("b1lo", [NEXP, DM], BF16)
    k.b2hi = scr("b2hi", [NEXP, DM], BF16)
    k.b2lo = scr("b2lo", [NEXP, DM], BF16)
    k.x1_kind = "Internal"
    k.xt_sp = scr("xt_sp", [NCH, 16, 128, 128], BF16)
    k.xc_sp = scr("xc_sp", [16, 128, L], BF16)
    k.zs_sp = scr("zs_sp", [NCH, 128, DI], BF16)
    k.sfin_sp = scr("sfin_sp", [NCH, 64, 2048], BF16)
    k.sb_sp = scr("sb_sp", [NCH, 64, 2048], BF16)
    import os as _os
    k.x1_sp = nc.dram_tensor(
        "x1_sp", [L, DM], F32,
        kind=("ExternalOutput" if _os.environ.get("X1DEBUG")
              else "Internal")).ap()
    k.pool = scr("pool", [L, DM], BF16)
    k.tfl = scr("tfl", [192, L], BF16)
    k.efl = scr("efl", [64, 128], BF16)
    k.lfl = scr("lfl", [128, 16], BF16)
    k.in_names.append("sorted_x")
    k.sorted_x = nc.dram_tensor("sorted_x", [NTIL * 128, DM], BF16,
                                kind="ExternalInput").ap()
    k.eo_sp = nc.dram_tensor(
        "eo_sp", [NTIL * 128, DM], F32,
        kind=("ExternalOutput" if _os.environ.get("MOEDEBUG")
              else "Internal")).ap()
    k.sh_cnt = nc.dram_tensor("sh_cnt", [NEXP + 3, NEXP], F32,
                              addr_space="Shared", kind="Internal").ap()
    k.sh_flag = nc.dram_tensor("sh_flag", [3, 16], I32,
                               addr_space="Shared", kind="Internal").ap()

    ctx = ExitStack()
    k.ctx = ctx

    class Sem:
        def __init__(self, name):
            self.h = ctx.enter_context(nc.semaphore(name))
            self.n = 0

    k.sd = Sem("sd")
    k.sg = Sem("sg")
    k.sa = Sem("sa")
    k.sv2 = Sem("sv2")
    k.sv = Sem("sv")
    k.sc = Sem("sc")
    k.sp = Sem("sp")
    k.sq = Sem("sq")
    k.sg2 = Sem("sg2")
    for _nm in ("z1", "z2", "z3", "z4", "z5", "z6", "z7", "z8",
                "z9", "za", "zb", "zc", "zd", "ze"):
        setattr(k, "s" + _nm, Sem("s" + _nm))
    k.all_sems = [k.sd, k.sg, k.sa, k.sv2, k.sv, k.sc, k.sp, k.sq,
                  k.sz1, k.sz2, k.sz3, k.sz4, k.sz5, k.sz6, k.sz7, k.sz8,
                  k.sz9, k.sza, k.szb, k.szc, k.szd, k.sze]

    DMA_SEM = {"sync": k.sd, "gpsimd": k.sg, "scalar": k.sa, "vector": k.sv2}
    CMP_SEM = {"vector": k.sv, "scalar": k.sc, "tensor": k.sp, "gpsimd": k.sq}
    k.last_wait = {n: {} for n in ("sync", "tensor", "vector", "scalar",
                                   "gpsimd")}

    def wait(eng_name, *sems_vals):
        best = {}
        for s, v in sems_vals:
            if v <= 0:
                continue
            if best.get(id(s), (None, -1))[1] < v:
                best[id(s)] = (s, v)
        pairs = []
        lw = k.last_wait[eng_name]
        for s, v in best.values():
            if lw.get(id(s), -1) >= v:
                continue
            lw[id(s)] = v
            pairs.append((s.h, v))
        if pairs:
            def fn(eng, pairs=pairs):
                for h, v in pairs:
                    eng.wait_ge(h, v)
            P.steps.append((eng_name, fn))

    def dma(queue, out, in_, sem=None, **kw):
        s = sem if sem is not None else DMA_SEM[queue]
        s.n += 16

        def fn(eng, out=out, in_=in_, kw=kw):
            eng.dma_start(out=out, in_=in_, **kw).then_inc(s.h, 16)
        P.steps.append((queue, fn))
        return (s, s.n)

    def comp(eng_name, emit):
        s = CMP_SEM[eng_name]
        s.n += 1

        def fn(eng, emit=emit):
            emit(eng).then_inc(s.h, 1)
        P.steps.append((eng_name, fn))
        return (s, s.n)

    def raw(eng_name, emit):
        P.steps.append((eng_name, lambda eng, emit=emit: emit(eng)))

    def barrier():
        allv = [(s, s.n) for s in k.all_sems] + [(k.sg2, k.sg2.n)]
        for en in ("sync", "tensor", "vector", "scalar", "gpsimd"):
            wait(en, *allv)

    def tap(name, src, queue="sync"):
        if name in k.dbg:
            wait(queue, *[(sem, sem.n) for sem in k.all_sems + [k.sg2]])
            t = dma(queue, k.dbg[name], src)
            wait(queue, t)

    def vf():
        wait("vector", (k.sv, k.sv.n))

    k.wait, k.dma, k.comp, k.raw = wait, dma, comp, raw
    k.barrier, k.tap, k.vf = barrier, tap, vf
    return k


def prep(k):
    nc, e = k.nc, k.ctx.enter_context
    wait, dma, comp, raw, vf = k.wait, k.dma, k.comp, k.raw, k.vf

    k.wtoB = e(_sb(nc, "wtoB", [128, 16 * DM], BF16))
    k.fcdB = e(_sb(nc, "fcdB", [128, 16 * NH], BF16))
    k.wrB = e(_sb(nc, "wrB", [128, 8 * NEXP], BF16))
    k.convD = e(_sb(nc, "convD", [128, 126 * 128], BF16))
    k.convB = e(_sb(nc, "convB", [128, 18], F32))
    k.identB = e(_sb(nc, "identB", [128, 128], BF16))
    k.identF = e(_sb(nc, "identF", [128, 128], F32))
    k.maskfB = e(_sb(nc, "maskfB", [128, 128], BF16))
    k.maskbB = e(_sb(nc, "maskbB", [128, 128], BF16))
    k.triliB = e(_sb(nc, "triliB", [128, 128], BF16))
    k.sel2h4B = e(_sb(nc, "sel2h4B", [4, 128], BF16))
    k.iota8B = e(_sb(nc, "iota8B", [128, NEXP], F32))
    k.iotakB = e(_sb(nc, "iotakB", [128, 8], F32))
    k.tilestB = e(_sb(nc, "tilestB", [NEXP, NTIL], F32))
    k.onescbB = e(_sb(nc, "onescbB", [128, 1], BF16))
    k.onesrbB = e(_sb(nc, "onesrbB", [1, 128], BF16))
    k.ones2bB = e(_sb(nc, "ones2bB", [2, 128], BF16))
    k.onesrfB = e(_sb(nc, "onesrfB", [1, 512], F32))
    k.dtbB = e(_sb(nc, "dtbB", [64, 1], F32))
    k.acolB = e(_sb(nc, "acolB", [64, 1], F32))
    k.dparrB = e(_sb(nc, "dparrB", [1, NH], F32))
    k.brB = e(_sb(nc, "brB", [1, NEXP], BF16))
    k.prefmB = e(_sb(nc, "prefmB", [NEXP, 1], F32))
    k.validB = e(_sb(nc, "validB", [NEXP, NEXP], I32))
    k.vrowsB = e(_sb(nc, "vrowsB", [NEXP, 1], I32))
    k.zrowB = e(_sb(nc, "zrowB", [1, 128], BF16))
    k.ones512b = e(_sb(nc, "ones512b", [1, 512], BF16))
    k.zer88 = e(_sb(nc, "zer88", [NEXP, NEXP], F32))
    k.cvalB = e(_sb(nc, "cvalB", [128, 8], F32))
    for _i, _v in enumerate((1e-5, 1.0, -1.0, -0.5, 1.0 / 2048.0)):
        nc.const_aps.register(F32, _v, k.cvalB[:, _i:_i + 1])
    k.jobrB = e(_sb(nc, "jobrB", [NJOBS, 1], I32))
    k.parrB = e(_sb(nc, "parrB", [2, 1], I32))
    k.cntsB = e(_sb(nc, "cntsB", [NJOBS, NEXP], F32))
    k.lpcols = e(_sb(nc, "lpcols", [128, NCH], F32))
    k.gcols = e(_sb(nc, "gcols", [128, NCH], F32))
    k.icols = e(_sb(nc, "icols", [128, NCH], F32))
    k.erhs = [e(_sb(nc, f"erhs{i}", [99, 4096], BF16)) for i in (0, 1)]

    loads = []
    for dst, src in ((k.identB, k.identb), (k.identF, k.identf),
                     (k.maskfB, k.maskf),
                     (k.maskbB, k.maskb), (k.triliB, k.trili),
                     (k.sel2h4B, k.sel2h4), (k.iota8B, k.iota8r),
                     (k.iotakB, k.iotak), (k.tilestB, k.tilest),
                     (k.onescbB, k.onescb), (k.onesrbB, k.onesrb),
                     (k.ones2bB, k.ones2b), (k.onesrfB, k.onesrf),
                     (k.dtbB, k.dtb2), (k.dparrB, k.dparr),
                     (k.prefmB, k.prefm), (k.validB, k.validm),
                     (k.vrowsB, k.vrows),
                     (k.jobrB, k.jobrows),
                     (k.parrB, k.parrow), (k.convB, k.conv_bc),
                     (k.cvalB, k.cvals)):
        loads.append(dma("sync", dst[:, :], src))
    for i in (0, 1):
        loads.append(dma("sync", k.erhs[i][0:96, :], k.sel3))
    t_small = loads[-1]

    with _sb(nc, "pf32", [128, FIP], F32) as pf32, \
         _sb(nc, "pbf", [128, FIP], BF16) as pbf, \
         _sb(nc, "pf32g", [128, DM], F32) as pf32g, \
         _sb(nc, "pbfg", [128, DM], BF16) as pbfg, \
         _sb(nc, "alog_t", [64, 1], F32) as alog_t, \
         _sb(nc, "convW", [128, 126], F32) as convW, \
         _sb(nc, "brf", [1, NEXP], F32) as brf, \
         _sb(nc, "nwcB", [128, 16], F32) as nwcB, \
         _sb(nc, "bsp", [NEXP, DM], F32) as bsp, \
         _sb(nc, "bhi", [NEXP, DM], BF16) as bhi, \
         _sb(nc, "blo", [NEXP, DM], BF16) as blo:
        dma("sync", alog_t[:, :], k.alog2)
        dma("sync", brf[:, :], k.br)
        dma("sync", convW[:, :], k.conv_wc)
        tinit = dma("sync", nwcB[:, :], k.nwc)
        wait("scalar", tinit)
        c1 = comp("scalar", lambda s: s.activation(
            out=k.acolB[:, :], in_=alog_t[:, :], func=AF.Exp))
        wait("vector", c1)
        comp("vector", lambda v: v.tensor_scalar_mul(
            k.acolB[:, :], k.acolB[:, :], -1.0))
        wait("vector", tinit)
        comp("vector", lambda v: v.tensor_copy(out=k.brB[:, :],
                                               in_=brf[:, :]))
        for idx in range(126):
            raw("vector", lambda v, idx=idx: v.tensor_scalar(
                out=k.convD[:, idx * 128:(idx + 1) * 128],
                in0=k.identB[:, :], scalar1=convW[:, idx:idx + 1],
                scalar2=None, op0=ALU.mult))
        comp("vector", lambda v: v.memset(k.zer88[:, :], 0.0))
        comp("vector", lambda v: v.memset(k.cntsB[:, :], 0.0))
        comp("vector", lambda v: v.memset(k.zrowB[:, :], 0.0))
        comp("vector", lambda v: v.memset(k.ones512b[:, :], 1.0))
        comp("vector", lambda v: v.memset(k.erhs[0][96:99, :], 1.0))
        comp("vector", lambda v: v.memset(k.erhs[1][96:99, :], 1.0))

        for i in range(16):
            wait("sync", (k.sv, k.sv.n), (k.sc, k.sc.n))
            t1 = dma("sync", pf32[:, 0:DM], k.wt_out[i * 128:(i + 1) * 128, :],
                     sem=k.sz1)
            wait("vector", t1)
            comp("vector", lambda v, i=i: v.tensor_scalar(
                out=k.wtoB[:, i * DM:(i + 1) * DM], in0=pf32[:, 0:DM],
                scalar1=nwcB[:, i:i + 1], scalar2=None, op0=ALU.mult))
        for i in range(16):
            wait("sync", (k.sv, k.sv.n))
            t1 = dma("sync", pf32[:, 0:NH], k.fcdt[i * 128:(i + 1) * 128, :],
                     sem=k.sz1)
            wait("vector", t1)
            comp("vector", lambda v, i=i: v.tensor_copy(
                out=k.fcdB[:, i * NH:(i + 1) * NH], in_=pf32[:, 0:NH]))
        for i in range(8):
            wait("sync", (k.sv, k.sv.n))
            t1 = dma("sync", pf32[:, 0:NEXP], k.wr[i * 128:(i + 1) * 128, :],
                     sem=k.sz1)
            wait("vector", t1)
            comp("vector", lambda v, i=i: v.tensor_copy(
                out=k.wrB[:, i * NEXP:(i + 1) * NEXP], in_=pf32[:, 0:NEXP]))
        for i in range(8):
            wait("sync", (k.sv, k.sv.n), (k.sa, k.sa.n), (k.sc, k.sc.n))
            t1 = dma("sync", pf32[:, :], k.wt_in[i * 128:(i + 1) * 128, :],
                     sem=k.sz1)
            wait("scalar", t1)
            c2 = comp("scalar", lambda s: s.copy(out=pbf[:, :], in_=pf32[:, :]))
            wait("scalar", c2)
            td = dma("scalar", k.wtin_bf[i * 128:(i + 1) * 128, :], pbf[:, :])
            wait("scalar", td)
        for src, dst in ((k.w1f, k.w1bf), (k.w2f, k.w2bf)):
            for i in range(64):
                t1 = dma("gpsimd", pf32g[:, :], src[i * 128:(i + 1) * 128, :])
                wait("gpsimd", t1)
                c2 = comp("gpsimd", lambda g: g.tensor_copy(
                    out=pbfg[:, :], in_=pf32g[:, :]))
                wait("gpsimd", c2)
                td = dma("gpsimd", dst[i * 128:(i + 1) * 128, :], pbfg[:, :])
                wait("gpsimd", td)
        for src, dhi, dlo in ((k.b1r, k.b1hi, k.b1lo),
                              (k.b2r, k.b2hi, k.b2lo)):
            wait("sync", (k.sv, k.sv.n), (k.sa, k.sa.n))
            t1 = dma("sync", bsp[:, :], src, sem=k.sz1)
            wait("vector", t1)
            ch = comp("vector", lambda v: v.tensor_copy(out=bhi[:, :],
                                                        in_=bsp[:, :]))
            vf()
            cm = comp("vector", lambda v: v.tensor_tensor(
                out=bsp[:, :], in0=bsp[:, :], in1=bhi[:, :], op=ALU.subtract))
            vf()
            c2 = comp("vector", lambda v: v.tensor_copy(out=blo[:, :],
                                                        in_=bsp[:, :]))
            wait("scalar", c2)
            d1 = dma("scalar", dhi, bhi[:, :])
            d2 = dma("scalar", dlo, blo[:, :])
            wait("vector", d1, d2)
        k.barrier()


def attn_job(k, job):
    nc, e = k.nc, k.ctx.enter_context
    wait, dma, comp, raw, tap = k.wait, k.dma, k.comp, k.raw, k.tap
    vf = k.vf
    own = (job == 0)
    u = k.u_jobs[job * L:(job + 1) * L, :]

    with ExitStack() as S:
        es = S.enter_context
        bcpad = es(_sb(nc, "bcpad", [128, 2 * (L + 2)], BF16))
        dtr = es(_sb(nc, "dtr", [64, L], F32))
        tbuf = es(_sb(nc, "tbuf", [64, L], F32))
        rbuf = es(_sb(nc, "rbuf", [64, L], F32))
        rpack = [es(_sb(nc, f"rpack{d}", [99, L], BF16))
                 for d in (0, 1)]
        wbias = es(_sb(nc, "wbias", [64, 16], F32))
        biasfm = es(_sb(nc, "biasfm", [64, 16], F32))
        fcdTm = es(_sb(nc, "fcdTm", [128, 16 * NH], F32))

        SA = ExitStack()
        ea = SA.enter_context
        ulnT = ea(_sb(nc, "ulnT", [128, 8 * L], BF16))
        x0t = ea(_sb(nc, "x0t", [128, DM], F32))
        ulnt = ea(_sb(nc, "ulnt", [128, DM], BF16))
        stat = ea(_sb(nc, "stat", [128, 12], F32))
        mv = ea(_sb(nc, "mv", [128, 2], F32))
        rstd = ea(_sb(nc, "rstd", [128, 1], F32))
        tstage = ea(_sb(nc, "tstage", [128, 512], BF16))
        SAz = ExitStack()
        wtZ = SAz.enter_context(_sb(nc, "wtZ", [128, 8 * DI], BF16))
        zst = SAz.enter_context(_sb(nc, "zst", [128, DI], BF16))

        cpad = comp("vector", lambda v: v.memset(bcpad[:, :], 0.0))
        wait("tensor", cpad)
        wait("scalar", cpad)

        tz = []
        for kc in range(8):
            tz.append(dma("sync", wtZ[:, kc * DI:(kc + 1) * DI],
                          k.wtin_bf[kc * 128:(kc + 1) * 128, 0:DI],
                          sem=k.sz2))

        with _ps(nc, "ptpA", [128, 512], BF16) as ptpA:
            for c in range(NCH):
                wait("sync", (k.sv, k.sv.n))
                t1 = dma("sync", x0t[:, :], u[c * 128:(c + 1) * 128, :],
                         sem=k.sz3)
                wait("vector", t1, (k.sp, k.sp.n))
                raw("vector", lambda v: v.bn_stats(out=stat[:, 0:6],
                                                   in_=x0t[:, 0:512]))
                comp("vector", lambda v: v.bn_stats(out=stat[:, 6:12],
                                                    in_=x0t[:, 512:1024]))
                vf()
                c1 = comp("vector", lambda v: v.bn_aggr(out=mv[:, :],
                                                        in_=stat[:, :]))
                wait("scalar", c1)
                c1b = comp("scalar", lambda s: s.activation(
                    out=rstd[:, :], in_=mv[:, 1:2], func=AF.Ln, bias=1e-5))
                wait("scalar", c1b)
                c2 = comp("scalar", lambda s: s.activation(
                    out=rstd[:, :], in_=rstd[:, :], func=AF.Exp, scale=-0.5))
                wait("vector", c2)
                c3 = comp("vector", lambda v: v.tensor_scalar(
                    out=ulnt[:, :], in0=x0t[:, :], scalar1=mv[:, 0:1],
                    scalar2=rstd[:, :], op0=ALU.subtract, op1=ALU.mult))
                wait("tensor", c3)
                for g4 in range(2):
                    for k2 in range(4):
                        kc = g4 * 4 + k2
                        em = (lambda t, kc=kc, k2=k2: t.transpose(
                            ptpA[:, k2 * 128:(k2 + 1) * 128],
                            ulnt[:, kc * 128:(kc + 1) * 128], k.identB[:, :]))
                        if k2 == 3:
                            c5 = comp("tensor", em)
                        else:
                            raw("tensor", em)
                    wait("scalar", c5)
                    dstv = sview(ulnT[:, g4 * 4 * L + c * 128:],
                                 (L, 4), (1, 128))
                    c6 = comp("scalar", lambda s, dstv=dstv: s.copy(
                        out=dstv, in_=ptpA[:, 0:512].rearrange(
                            "p (a c) -> p a c", a=4)))
                    wait("tensor", c6)

        tap("ulnT", ulnT[:, :])
        if STOP_AFTER == "p1a":
            k.barrier()
            SAz.close()
            SA.close()
            return

        with _ps(nc, "psZ", [128, 512], F32) as psZ:
            wait("tensor", *tz)
            for c in range(NCH):
                for zb in range(4):
                    for kc in range(8):
                        em = (lambda t, c=c, zb=zb, kc=kc: t.matmul(
                            psZ[:, :],
                            ulnT[:, kc * L + c * 128:kc * L + (c + 1) * 128],
                            wtZ[:, kc * DI + zb * 512:kc * DI + (zb + 1) * 512],
                            start=(kc == 0), stop=(kc == 7)))
                        if kc == 7:
                            c1 = comp("tensor", em)
                        else:
                            raw("tensor", em)
                    wait("scalar", c1)
                    c2 = comp("scalar", lambda s, zb=zb: s.activation(
                        out=zst[:, zb * 512:(zb + 1) * 512], in_=psZ[:, :],
                        func=AF.Silu))
                    wait("tensor", c2)
                wait("scalar", c2)
                t3 = dma("scalar", k.zs_sp[c], zst[:, :])
                wait("scalar", t3)

        k.barrier()
        SAz.close()
        if STOP_AFTER == "p1b":
            SA.close()
            return
        xbct = ea(_sb(nc, "xbct", [128, L + 6], BF16))
        xcv = ea(_sb(nc, "xcv", [128, L], BF16))
        wtS = ea(_sb(nc, "wtS", [128, 8 * 128], BF16))
        raw("vector", lambda v: v.memset(xbct[:, 0:3], 0.0))
        cxp = comp("vector", lambda v: v.memset(xbct[:, L + 3:L + 6], 0.0))
        wait("tensor", cxp)
        wait("scalar", cxp)
        with _ps(nc, "psF", [128, 512], F32) as psF, \
             _ps(nc, "psC", [128, 512], F32) as psC, \
             _ps(nc, "ptpB", [128, 512], BF16) as ptpB:
            for j in range(16, 35):
                jc = j - 16
                fw = 128 if j < 34 else 64
                wait("sync", (k.sp, k.sp.n))
                tw = []
                for kc in range(8):
                    tw.append(dma("sync", wtS[:, kc * 128:kc * 128 + fw],
                                  k.wtin_bf[kc * 128:(kc + 1) * 128,
                                            j * 128:j * 128 + fw],
                                  sem=k.sz4))
                wait("tensor", *tw)
                for tb in range(4):
                    for kc in range(8):
                        em = (lambda t, tb=tb, kc=kc, fw=fw: t.matmul(
                            psF[0:fw, :], wtS[:, kc * 128:kc * 128 + fw],
                            ulnT[:, kc * L + tb * 512:kc * L + (tb + 1) * 512],
                            start=(kc == 0), stop=(kc == 7)))
                        if kc == 7:
                            c1 = comp("tensor", em)
                        else:
                            raw("tensor", em)
                    if j < 34:
                        wait("scalar", c1)
                        c2 = comp("scalar", lambda s, tb=tb: s.copy(
                            out=xbct[:, 3 + tb * 512:3 + (tb + 1) * 512],
                            in_=psF[:, :]))
                        wait("tensor", c2)
                    else:
                        wait("vector", c1)
                        c2 = comp("vector", lambda v, tb=tb: v.tensor_copy(
                            out=dtr[:, tb * 512:(tb + 1) * 512],
                            in_=psF[0:64, :]))
                        wait("tensor", c2)
                if j == 34:
                    break
                for tb in range(4):
                    for tp in range(7):
                        em = (lambda t, jc=jc, tb=tb, tp=tp: t.matmul(
                            psC[:, :],
                            k.convD[:, (jc * 7 + tp) * 128:(jc * 7 + tp + 1) * 128],
                            xbct[:, tp + tb * 512:tp + (tb + 1) * 512],
                            start=(tp == 0), stop=(tp == 6)))
                        if tp == 6:
                            c1 = comp("tensor", em)
                        else:
                            raw("tensor", em)
                    wait("scalar", c1)
                    c2 = comp("scalar", lambda s, jc=jc, tb=tb: s.activation(
                        out=xcv[:, tb * 512:(tb + 1) * 512], in_=psC[:, :],
                        func=AF.Silu, bias=k.convB[:, jc:jc + 1]))
                    wait("tensor", c2)
                if j == 32:
                    tap("xbct32", xbct[:, :])
                    tap("xcv32", xcv[:, :])
                if jc <= 15:
                    wait("scalar", c2)
                    t3 = dma("scalar", k.xc_sp[jc], xcv[:, :])
                    wait("scalar", t3)
                    for tt0 in range(0, 16, 4):
                        for k2 in range(4):
                            em = (lambda t, tt0=tt0, k2=k2: t.transpose(
                                ptpB[:, k2 * 128:(k2 + 1) * 128],
                                xcv[:, (tt0 + k2) * 128:(tt0 + k2 + 1) * 128],
                                k.identB[:, :]))
                            if k2 == 3:
                                c4 = comp("tensor", em)
                            else:
                                raw("tensor", em)
                        wait("scalar", c4)
                        c5 = comp("scalar", lambda s: s.copy(
                            out=tstage[:, :], in_=ptpB[:, 0:512]))
                        wait("tensor", c5)
                        wait("scalar", c5)
                        t6 = dma("scalar",
                                 k.xt_sp[tt0:tt0 + 4, jc].rearrange(
                                     "a p c -> p a c"),
                                 tstage[:, :].rearrange("p (a c) -> p a c",
                                                        a=4))
                        wait("scalar", t6)
                else:
                    r = jc - 16
                    wait("vector", c2)
                    c4 = comp("vector", lambda v, r=r: v.tensor_copy(
                        out=bcpad[:, r * (L + 2) + 1:r * (L + 2) + 1 + L],
                        in_=xcv[:, :]))
                    wait("scalar", c4)

        tap("dtr", dtr[:, :])
        tap("bcpad", bcpad[:, :])
        if STOP_AFTER == "p1c":
            k.barrier()
            SA.close()
            return

        with _ps(nc, "pfcd", [NH, L], F32) as pfcd, \
             _ps(nc, "ptpC", [128, NH], F32) as ptpC, \
             _sb(nc, "fcdF", [NH, L], F32) as fcdF:
            for jc in range(16):
                wait("sync", (k.sp, k.sp.n))
                t1 = dma("sync", xcv[:, :], k.xc_sp[jc], sem=k.sz5)
                wait("tensor", t1)
                for tb in range(4):
                    em = (lambda t, jc=jc, tb=tb: t.matmul(
                        pfcd[:, tb * 512:(tb + 1) * 512],
                        k.fcdB[:, jc * NH:(jc + 1) * NH],
                        xcv[:, tb * 512:(tb + 1) * 512],
                        start=(jc == 0), stop=False))
                    if tb == 3:
                        comp("tensor", em)
                    else:
                        raw("tensor", em)
            for nb in range(4):
                em = (lambda t, nb=nb: t.matmul(
                    pfcd[:, nb * 512:(nb + 1) * 512],
                    k.dparrB[:, :],
                    k.onesrfB[:, :],
                    start=False, stop=True))
                if nb == 3:
                    c1 = comp("tensor", em)
                else:
                    raw("tensor", em)
            wait("vector", c1)
            c2 = comp("vector", lambda v: v.tensor_copy(out=fcdF[:, :],
                                                        in_=pfcd[:, :]))
            wait("tensor", c2)
            for tt in range(16):
                c3 = comp("tensor", lambda t, tt=tt: t.transpose(
                    ptpC[:, :], fcdF[:, tt * 128:(tt + 1) * 128],
                    k.identF[0:32, 0:32]))
                wait("vector", c3)
                c4 = comp("vector", lambda v, tt=tt: v.tensor_copy(
                    out=fcdTm[:, tt * NH:(tt + 1) * NH], in_=ptpC[:, :]))
                wait("tensor", c4)
        tap("fcdTm", fcdTm[:, :])
        k.barrier()
        SA.close()
        if STOP_AFTER == "p1":
            return

        SB = ExitStack()
        eb = SB.enter_context
        dts = eb(_sb(nc, "dts", [64, L], F32))
        tmpa = eb(_sb(nc, "tmpa", [64, L], F32))
        lndt = eb(_sb(nc, "lndt", [64, L], F32))
        ab = eb(_sb(nc, "ab", [64, L], F32))
        incl = eb(_sb(nc, "incl", [64, L], F32))
        acsb = eb(_sb(nc, "acsb", [64, L], F32))
        zero64 = eb(_sb(nc, "zero64", [64, L], F32))
        shi = eb(_sb(nc, "shi", [64, L], BF16))
        smid = eb(_sb(nc, "smid", [64, L], BF16))
        slo = eb(_sb(nc, "slo", [64, L], BF16))
        bnd = eb(_sb(nc, "bnd", [64, 17], F32))
        lam = eb(_sb(nc, "lam", [64, 16], F32))
        lamsp = eb(_sb(nc, "lamsp", [64, 16], BF16))

        c0 = comp("vector", lambda v: v.memset(zero64[:, :], 0.0))
        cs1 = comp("scalar", lambda s: s.activation(
            out=dts[:, :], in_=dtr[:, :], func=AF.Identity,
            bias=k.dtbB[:, :]))
        cs2 = comp("scalar", lambda s: s.activation(
            out=tmpa[:, :], in_=dtr[:, :], func=AF.Abs, bias=k.dtbB[:, :]))
        wait("scalar", cs2)
        cs3 = comp("scalar", lambda s: s.activation(
            out=tmpa[:, :], in_=tmpa[:, :], func=AF.Exp, scale=-1.0))
        wait("scalar", cs3)
        cs4 = comp("scalar", lambda s: s.activation(
            out=tmpa[:, :], in_=tmpa[:, :], func=AF.Ln, bias=1.0))
        wait("vector", cs1, cs4)
        cv1 = comp("vector", lambda v: v.tensor_scalar_max(
            dts[:, :], dts[:, :], 0.0))
        wait("vector", cv1)
        cv2 = comp("vector", lambda v: v.tensor_add(
            dts[:, :], dts[:, :], tmpa[:, :]))
        wait("scalar", cv2)
        comp("scalar", lambda s: s.activation(out=lndt[:, :], in_=dts[:, :],
                                              func=AF.Ln))
        wait("vector", cv2)
        cv3 = comp("vector", lambda v: v.tensor_scalar(
            out=ab[:, :], in0=dts[:, :], scalar1=k.acolB[:, :], scalar2=None,
            op0=ALU.mult))
        wait("vector", cv3, c0)
        cv4 = comp("vector", lambda v: v.tensor_tensor_scan(
            out=incl[:, :], data0=ab[:, :], data1=zero64[:, :], initial=0.0,
            op0=ALU.add, op1=ALU.add))
        wait("vector", cv4)
        cv5 = comp("vector", lambda v: v.tensor_copy(out=acsb[:, :],
                                                     in_=incl[:, :]))
        wait("vector", cv5)
        cv6 = comp("vector", lambda v: v.tensor_tensor(
            out=acsb[32:64, :], in0=incl[32:64, :], in1=ab[32:64, :],
            op=ALU.subtract))
        raw("vector", lambda v: v.memset(tbuf[0:32, 0:1], 0.0))
        wait("vector", cv6)
        raw("vector", lambda v: v.tensor_copy(out=tbuf[0:32, 1:L],
                                              in_=acsb[0:32, 0:L - 1]))
        raw("vector", lambda v: v.tensor_scalar_mul(
            tbuf[32:64, 0:L - 1], acsb[32:64, 1:L], -1.0))
        cv7 = comp("vector", lambda v: v.tensor_scalar_mul(
            tbuf[32:64, L - 1:L], incl[32:64, L - 1:L], -1.0))
        wc1 = (k.sc, k.sc.n)
        wait("vector", wc1)
        raw("vector", lambda v: v.tensor_tensor(
            out=rbuf[0:32, :], in0=lndt[0:32, :], in1=acsb[0:32, :],
            op=ALU.subtract))
        cv8 = comp("vector", lambda v: v.tensor_tensor(
            out=rbuf[32:64, :], in0=lndt[32:64, :], in1=acsb[32:64, :],
            op=ALU.add))
        wait("vector", cv8)

        def split3(src, reuse):
            comp("vector", lambda v: v.tensor_copy(out=shi[:, :],
                                                   in_=src[:, :]))
            vf()
            comp("vector", lambda v: v.tensor_tensor(
                out=tmpa[:, :], in0=src[:, :], in1=shi[:, :],
                op=ALU.subtract))
            vf()
            comp("vector", lambda v: v.tensor_copy(out=smid[:, :],
                                                   in_=tmpa[:, :]))
            vf()
            comp("vector", lambda v: v.tensor_tensor(
                out=tmpa[:, :], in0=tmpa[:, :], in1=smid[:, :],
                op=ALU.subtract))
            vf()
            return comp("vector", lambda v: v.tensor_copy(out=slo[:, :],
                                                          in_=tmpa[:, :]))

        cr = split3(rbuf, None)
        wait("vector", cr)
        raw("vector", lambda v: v.memset(rpack[0][96:99, :], 1.0))
        crm = comp("vector", lambda v: v.memset(rpack[1][96:99, :], 1.0))
        wait("sync", cr, crm)
        rp = []
        for d in (0, 1):
            rp.append(dma("sync", rpack[d][0:32, :],
                          shi[d * 32:(d + 1) * 32, :], sem=k.szd))
            rp.append(dma("sync", rpack[d][32:64, :],
                          smid[d * 32:(d + 1) * 32, :], sem=k.szd))
            rp.append(dma("sync", rpack[d][64:96, :],
                          slo[d * 32:(d + 1) * 32, :], sem=k.szd))
        wait("vector", *rp)
        ct = split3(tbuf, None)
        wait("sync", ct)
        tp1 = dma("sync", k.tfl[0:64, :], shi[:, :], sem=k.szd)
        tp2 = dma("sync", k.tfl[64:128, :], smid[:, :], sem=k.szd)
        tp3 = dma("sync", k.tfl[128:192, :], slo[:, :], sem=k.szd)
        wait("vector", tp1, tp2, tp3)

        raw("vector", lambda v: v.memset(bnd[:, 0:1], 0.0))
        raw("vector", lambda v: v.tensor_copy(
            out=bnd[0:32, 1:17], in_=sview(acsb[0:32, 127:], (128, 16))))
        raw("vector", lambda v: v.tensor_copy(
            out=bnd[32:64, 1:16], in_=sview(acsb[32:64, 128:], (128, 15))))
        cb1 = comp("vector", lambda v: v.tensor_copy(
            out=bnd[32:64, 16:17], in_=incl[32:64, L - 1:L]))
        wait("vector", cb1)
        raw("vector", lambda v: v.tensor_copy(out=wbias[0:32, :],
                                              in_=bnd[0:32, 1:17]))
        raw("vector", lambda v: v.tensor_scalar_mul(
            wbias[32:64, :], bnd[32:64, 0:16], -1.0))
        raw("vector", lambda v: v.tensor_scalar_mul(
            biasfm[0:32, :], bnd[0:32, 0:16], -1.0))
        raw("vector", lambda v: v.tensor_copy(out=biasfm[32:64, :],
                                              in_=bnd[32:64, 1:17]))
        cb2 = comp("vector", lambda v: v.tensor_tensor(
            out=lam[:, :], in0=bnd[:, 1:17], in1=bnd[:, 0:16],
            op=ALU.subtract))
        wait("scalar", cb2)
        cb3 = comp("scalar", lambda s: s.activation(out=lam[:, :],
                                                    in_=lam[:, :],
                                                    func=AF.Exp))
        wait("vector", cb3)
        comp("vector", lambda v: v.tensor_copy(out=lamsp[:, :], in_=lam[:, :]))
        vf()
        cb4 = comp("vector", lambda v: v.tensor_tensor(
            out=lam[:, :], in0=lam[:, :], in1=lamsp[:, :], op=ALU.subtract))
        wait("sync", cb4)
        lp1 = dma("sync", k.lfl[0:64, :], lamsp[:, :], sem=k.szd)
        wait("vector", lp1)
        cb5 = comp("vector", lambda v: v.tensor_copy(out=lamsp[:, :],
                                                     in_=lam[:, :]))
        wait("sync", cb5)
        lp2 = dma("sync", k.lfl[64:128, :], lamsp[:, :], sem=k.szd)
        wait("sync", lp2)
        tap("dts", dts[:, :])
        tap("ab", ab[:, :])
        tap("acolB", k.acolB[:, :])
        tap("dtbB", k.dtbB[:, :])
        tap("acsb", acsb[:, :])
        tap("tbuf", tbuf[:, :])
        tap("rbuf", rbuf[:, :])
        tap("bnd", bnd[:, :])
        k.barrier()
        SB.close()
        if STOP_AFTER == "p2":
            return

        SC = ExitStack()
        ec = SC.enter_context
        xtm = ec(_sb(nc, "xtm", [128, DI], BF16))
        wfm = ec(_sb(nc, "wfm", [64, 128], F32))
        wtm = ec(_sb(nc, "wtm", [128, 64], F32))
        btm = ec(_sb(nc, "btm", [128, 64], BF16))
        xpp = ec(_sb(nc, "xpp", [128, DI], BF16))
        slocp = ec(_sb(nc, "slocp", [128, 1024], F32))
        sacc = ec(_sb(nc, "sacc", [128, 1024], F32))
        stmp = ec(_sb(nc, "stmp", [128, 1024], F32))
        sbb = ec(_sb(nc, "sbb", [128, 1024], BF16))
        lam4 = ec(_sb(nc, "lam4", [4, 16], BF16))
        lrep = ec(_sb(nc, "lrep", [4, 1024], BF16))

        czs = comp("vector", lambda v: v.memset(sacc[:, :], 0.0))

        def lam_mm(d, c, psL):
            wait("sync", (k.sv, k.sv.n))
            for sp in (0, 1):
                src = bass.AP(k.lfl.tensor, sp * 1024 + d * 32 * 16 + c,
                              [[16, 2], [2 * 16, 16]])
                t1 = dma("sync", lam4[2 * sp:2 * sp + 2, :], src, sem=k.sz6,
                         allow_slow_non_contiguous=True)
            wait("vector", t1, (k.sp, k.sp.n))
            c1 = comp("vector", lambda v: v.tensor_copy(
                out=lrep[:, :], in_=sview(lam4[:, 0:], (1, 16), (0, 64))))
            wait("tensor", c1)
            raw("tensor", lambda t: t.matmul(
                psL[:, 0:512], k.sel2h4B[:, :], lrep[:, 0:512],
                start=True, stop=True))
            return comp("tensor", lambda t: t.matmul(
                psL[:, 512:1024], k.sel2h4B[:, :], lrep[:, 512:1024],
                start=True, stop=True))

        with _ps(nc, "psS", [64, DI], F32) as psS, \
             _ps(nc, "psL", [128, 1024], F32) as psL, \
             _ps(nc, "ptpD", [128, 64], F32) as ptpD, \
             _ps(nc, "ptpE", [128, 64], BF16) as ptpE:
            for c in range(NCH):
                wait("sync", (k.sv, k.sv.n), (k.sp, k.sp.n))
                tx = []
                for jc in range(16):
                    tx.append(dma("sync", xtm[:, jc * 128:(jc + 1) * 128],
                                  k.xt_sp[c, jc], sem=k.sz7))
                wait("scalar", (k.sp, k.sp.n))
                cw = comp("scalar", lambda s, c=c: s.activation(
                    out=wfm[:, :], in_=rbuf[:, c * 128:(c + 1) * 128],
                    func=AF.Exp, bias=wbias[:, c:c + 1]))
                wait("tensor", cw)
                ct1 = comp("tensor", lambda t: t.transpose(
                    ptpD[:, :], wfm[:, :], k.identF[0:64, 0:64]))
                wait("vector", ct1)
                cv1 = comp("vector", lambda v: v.tensor_copy(out=wtm[:, :],
                                                             in_=ptpD[:, :]))
                wait("vector", *tx, cv1)
                for d in (0, 1):
                    cvx = comp("vector", lambda v, d=d: v.tensor_tensor(
                        out=xpp[:, :], in0=xtm[:, :],
                        in1=sview(wtm[:, d * 32:], (1, 32), (0, 64)),
                        op=ALU.mult))
                    ct2 = comp("tensor", lambda t, d=d, c=c: t.transpose(
                        ptpE[:, :],
                        bcpad[0:64, d * (L + 2) + 1 + c * 128:
                              d * (L + 2) + 1 + (c + 1) * 128],
                        k.identB[0:64, 0:64]))
                    wait("vector", ct2)
                    cv2 = comp("vector", lambda v: v.tensor_copy(
                        out=btm[:, :], in_=ptpE[:, :]))
                    wait("tensor", cv2, cvx, cv1)
                    for qv in range(4):
                        em = (lambda t, qv=qv: t.matmul(
                            psS[:, qv * 512:(qv + 1) * 512],
                            btm[:, :], xpp[:, qv * 512:(qv + 1) * 512],
                            start=True, stop=True))
                        if qv == 3:
                            ct3 = comp("tensor", em)
                        else:
                            raw("tensor", em)
                    wait("vector", ct3)
                    raw("vector", lambda v: v.tensor_copy(
                        out=slocp[0:64, :],
                        in_=sview(psS[0:64, 0:], (128, 16), (1, 64))))
                    cv3 = comp("vector", lambda v: v.tensor_copy(
                        out=slocp[64:128, :],
                        in_=sview(psS[0:64, 64:], (128, 16), (1, 64))))
                    if d == 0:
                        wait("vector", czs)
                        cv4 = comp("vector", lambda v: v.tensor_copy(
                            out=sbb[:, :], in_=sacc[:, :]))
                        wait("sync", cv4)
                        dma("sync", _spackap(k.sfin_sp, c, 0), sbb[0:64, :],
                            sem=k.sz8)
                        ts = dma("sync", _spackap(k.sfin_sp, c, 1),
                                 sbb[64:128, :], sem=k.sz8)
                        cl = lam_mm(0, c, psL)
                        wait("vector", cl, cv3, ts)
                        comp("vector", lambda v: v.tensor_tensor(
                            out=stmp[:, :], in0=sacc[:, :], in1=psL[:, :],
                            op=ALU.mult))
                        vf()
                        czs = comp("vector", lambda v: v.tensor_add(
                            sacc[:, :], stmp[:, :], slocp[:, :]))
                        wait("tensor", czs)
                    else:
                        wait("vector", cv3)
                        cv5 = comp("vector", lambda v: v.tensor_copy(
                            out=sbb[:, :], in_=slocp[:, :]))
                        wait("sync", cv5)
                        dma("sync", _spackap(k.sb_sp, c, 0), sbb[0:64, :],
                            sem=k.sz8)
                        tb2 = dma("sync", _spackap(k.sb_sp, c, 1),
                                  sbb[64:128, :], sem=k.sz8)
                        wait("vector", tb2)
                        wait("tensor", cv5)

            cz2 = comp("vector", lambda v: v.memset(stmp[:, :], 0.0))
            for c in range(NCH - 1, -1, -1):
                wait("sync", (k.sz8, k.sz8.n))
                dma("sync", sbb[0:64, :], _spackap(k.sb_sp, c, 0),
                    sem=k.sz8)
                t1 = dma("sync", sbb[64:128, :], _spackap(k.sb_sp, c, 1),
                         sem=k.sz8)
                wait("vector", t1, cz2)
                cv1 = comp("vector", lambda v: v.tensor_copy(
                    out=slocp[:, :], in_=sbb[:, :]))
                wait("vector", cv1)
                cv2 = comp("vector", lambda v: v.tensor_copy(
                    out=sbb[:, :], in_=stmp[:, :]))
                wait("sync", cv2)
                dma("sync", _spackap(k.sb_sp, c, 0), sbb[0:64, :],
                    sem=k.sz8)
                t2 = dma("sync", _spackap(k.sb_sp, c, 1), sbb[64:128, :],
                         sem=k.sz8)
                cl = lam_mm(1, c, psL)
                wait("vector", cl, t2)
                comp("vector", lambda v: v.tensor_tensor(
                    out=sacc[:, :], in0=stmp[:, :], in1=psL[:, :],
                    op=ALU.mult))
                vf()
                cz2 = comp("vector", lambda v: v.tensor_add(
                    stmp[:, :], sacc[:, :], slocp[:, :]))
                wait("tensor", cz2)
        k.barrier()
        SC.close()
        if STOP_AFTER == "p3":
            return

        SD = ExitStack()
        ed = SD.enter_context
        xtm2 = ed(_sb(nc, "xtm2", [128, DI], BF16))
        zst2 = ed(_sb(nc, "zst2", [128, DI], BF16))
        sfF = ed(_sb(nc, "sfF", [64, 2048], BF16))
        sfB = ed(_sb(nc, "sfB", [64, 2048], BF16))
        x0t2 = ed(_sb(nc, "x0t2", [128, DM], F32))
        msb = ed(_sb(nc, "msb", [128, 4096], BF16))
        cpp = ed(_sb(nc, "cpp", [64, 4096], BF16))
        etmp = ed(_sb(nc, "etmp", [128, 1024], BF16))
        gmsk = ed(_sb(nc, "gmsk", [128, 128], BF16))
        cpad0 = ed(_sb(nc, "cpad0", [64, 130], BF16))
        eflr = ed(_sb(nc, "eflr", [1, 1024], BF16))
        efm = ed(_sb(nc, "efm", [64, 128], F32))
        efmb = ed(_sb(nc, "efmb", [64, 128], BF16))
        yb = ed(_sb(nc, "yb", [128, DI], F32))
        ygated = ed(_sb(nc, "ygated", [128, DI], BF16))
        ygT = ed(_sb(nc, "ygT", [128, DI], BF16))
        x1b = ed(_sb(nc, "x1b", [128, DM], F32))
        uln2 = ed(_sb(nc, "uln2", [128, DM], BF16))
        u2T = ed(_sb(nc, "u2T", [128, DM], BF16))
        lgs = ed(_sb(nc, "lgs", [128, NEXP], F32))
        stat2 = ed(_sb(nc, "stat2", [128, 12], F32))
        mv2 = ed(_sb(nc, "mv2", [128, 2], F32))
        rcol = ed(_sb(nc, "rcol", [128, 1], F32))
        ngm = ed(_sb(nc, "ngm", [128, 1], F32))
        sexp = ed(_sb(nc, "sexp", [128, 1], F32))
        gcol = ed(_sb(nc, "gcol", [128, 1], F32))
        edump = ed(_sb(nc, "edump", [128, NEXP], F32))
        mx8 = ed(_sb(nc, "mx8", [128, NEXP], F32))
        mi8 = ed(_sb(nc, "mi8", [128, NEXP], U32))
        idxf = ed(_sb(nc, "idxf", [128, 1], F32))
        mskf = ed(_sb(nc, "mskf", [128, NEXP], F32))
        mskb16 = ed(_sb(nc, "mskb16", [128, NEXP], BF16))
        posb = ed(_sb(nc, "posb", [128, NEXP], F32))
        prefrow = ed(_sb(nc, "prefrow", [1, NEXP], F32))
        ctmp = ed(_sb(nc, "ctmp", [1, NEXP], F32))
        lpcols, gcols, icols = k.lpcols, k.gcols, k.icols
        poolr = ed(_sb(nc, "poolr", [128, DM], BF16))

        cpre = comp("vector", lambda v: v.memset(prefrow[:, :], 0.0))
        wait("tensor", cpre)

        with _ps(nc, "psY", [128, DI], F32) as psY, \
             _ps(nc, "psCnt", [1, NEXP], F32) as psCnt:
            for c in range(NCH):
                wait("sync", (k.sv, k.sv.n), (k.sp, k.sp.n))
                tx = [dma("sync", xtm2[:, jc * 128:(jc + 1) * 128],
                          k.xt_sp[c, jc], sem=k.sz9) for jc in range(16)]
                dma("sync", zst2[:, :], k.zs_sp[c], sem=k.sz9)
                dma("sync", sfF[:, :], k.sfin_sp[c], sem=k.sz9)
                dma("sync", sfB[:, :], k.sb_sp[c], sem=k.sz9)
                tx0 = dma("sync", x0t2[:, :], u[c * 128:(c + 1) * 128, :],
                          sem=k.sz9)
                tz2 = tsf = tsb = tx0

                wait("scalar", (k.sza, k.sza.n), (k.szb, k.szb.n))
                ce1 = comp("scalar", lambda s, c=c: s.activation(
                    out=efm[:, :], in_=tbuf[:, c * 128:(c + 1) * 128],
                    func=AF.Exp, bias=biasfm[:, c:c + 1]))
                wait("vector", ce1)
                ce2 = comp("vector", lambda v: v.tensor_copy(
                    out=efmb[:, :], in_=efm[:, :]))
                wait("sync", ce2)
                te = dma("sync", k.efl, efmb[:, :], sem=k.sza)

                for d in (0, 1):
                    coff = 0 if d == 0 else 2
                    vf()
                    cpw = comp("vector", lambda v, d=d, c=c: v.tensor_copy(
                        out=cpad0[:, :],
                        in_=bcpad[64:128, d * (L + 2) + c * 128:
                                  d * (L + 2) + c * 128 + 130]))
                    wait("sync", (k.sp, k.sp.n))
                    src = bass.AP(k.tfl.tensor, d * 32 * L + c * 128,
                                  [[64 * L, 3], [L, 32], [1, 128]])
                    tt = dma("sync", k.erhs[d][96:99, :], src, sem=k.szc)
                    with _ps(nc, "psG", [128, 128], F32) as psG:
                        wait("tensor", cpw)
                        cg1 = comp("tensor", lambda t, d=d, c=c,
                                   coff=coff: t.matmul(
                            psG[:, :],
                            bcpad[0:64, d * (L + 2) + 1 + c * 128:
                                  d * (L + 2) + 1 + (c + 1) * 128],
                            cpad0[:, coff:coff + 128],
                            start=True, stop=True))
                        wait("vector", cg1)
                        mk = k.maskfB if d == 0 else k.maskbB
                        cg2 = comp("vector", lambda v, mk=mk: v.tensor_tensor(
                            out=gmsk[:, :], in0=psG[:, :], in1=mk[:, :],
                            op=ALU.mult))
                        wait("tensor", cg2)
                    wait("vector", (k.sp, k.sp.n))
                    with _ps(nc, "psE", [128, 1024], F32) as psE:
                        for hg in range(4):
                            wait("tensor", tt, (k.sc, k.sc.n))
                            raw("tensor",
                                lambda t, d=d, c=c, hg=hg: t.matmul(
                                    psE[:, 0:512],
                                    rpack[d][:, c * 128:(c + 1) * 128],
                                    k.erhs[d][0:99,
                                              hg * 1024:hg * 1024 + 512],
                                    start=True, stop=True))
                            cm1 = comp("tensor",
                                       lambda t, d=d, c=c, hg=hg: t.matmul(
                                           psE[:, 512:1024],
                                           rpack[d][:, c * 128:(c + 1) * 128],
                                           k.erhs[d][0:99, hg * 1024 + 512:
                                                     (hg + 1) * 1024],
                                           start=True, stop=True))
                            wait("scalar", cm1)
                            cm2 = comp("scalar", lambda s: s.activation(
                                out=etmp[:, :], in_=psE[:, :], func=AF.Exp))
                            wait("vector", cm2, cg2)
                            cm3 = comp("vector",
                                       lambda v, hg=hg: v.tensor_tensor(
                                           out=msb[:, hg * 1024:(hg + 1) * 1024],
                                           in0=etmp[:, :],
                                           in1=sview(gmsk[:, 0:], (0, 8),
                                                     (1, 128)),
                                           op=ALU.mult))
                            wait("scalar", cm3)
                    wait("tensor", (k.sv, k.sv.n), tx0)
                    if d == 0:
                        for b4 in range(4):
                            raw("tensor", lambda t, b4=b4: t.matmul(
                                psY[:, b4 * 512:(b4 + 1) * 512],
                                k.zrowB[0:1, :], k.ones512b[0:1, :],
                                start=True, stop=False))
                    for h in range(32):
                        em = (lambda t, d=d, h=h: t.matmul(
                            psY[:, h * 64:(h + 1) * 64],
                            msb[:, (h // 8) * 1024 + (h % 8) * 128:
                                (h // 8) * 1024 + (h % 8) * 128 + 128],
                            xtm2[:, h * 64:(h + 1) * 64],
                            start=False, stop=False))
                        if h == 31:
                            cyd = comp("tensor", em)
                        else:
                            raw("tensor", em)
                    for qu in range(4):
                        wait("sync", (k.sv, k.sv.n), (k.sp, k.sp.n))
                        wait("vector", (k.sp, k.sp.n))
                        wait("sync", te)
                        tfr = dma("sync", eflr[:, :],
                                  bass.AP(k.efl.tensor,
                                          d * 32 * 128 + qu * 1024,
                                          [[128, 8], [1, 128]]),
                                  sem=k.szb)
                        wait("tensor", te, tfr)
                        with _ps(nc, "psX", [64, 1024], F32) as psX:
                            raw("tensor", lambda t: t.matmul(
                                psX[:, 0:512], k.onesrbB[0:1, 0:64],
                                eflr[0:1, 0:512],
                                start=True, stop=True))
                            cx1 = comp("tensor", lambda t: t.matmul(
                                psX[:, 512:1024], k.onesrbB[0:1, 0:64],
                                eflr[0:1, 512:1024],
                                start=True, stop=True))
                            wait("vector", cx1)
                            cx2 = comp("vector",
                                       lambda v, qu=qu, d=d, c=c,
                                       coff=coff: v.tensor_tensor(
                                           out=cpp[:, qu * 1024:
                                                   (qu + 1) * 1024],
                                           in0=sview(
                                               cpad0[:, coff:],
                                               (0, 8), (1, 128)),
                                           in1=psX[:, :], op=ALU.mult))
                            wait("tensor", cx2)
                    sf = sfF if d == 0 else sfB
                    wait("tensor", tsf if d == 0 else tsb, (k.sv, k.sv.n))
                    for h in range(32):
                        em = (lambda t, h=h, sf=sf, d=d: t.matmul(
                            psY[:, h * 64:(h + 1) * 64],
                            cpp[0:64, h * 128:(h + 1) * 128],
                            sf[0:64, h * 64:(h + 1) * 64],
                            start=False, stop=(d == 1 and h % 8 == 7)))
                        if h == 31:
                            cyo = comp("tensor", em)
                        else:
                            raw("tensor", em)

                if c == NCH - 1:
                    tap("msb", msb[:, :])
                    tap("gmsk", gmsk[:, :])
                    tap("cpp", cpp[:, :])
                    tap("sfF", sfF[:, :])
                    tap("sfB", sfB[:, :])
                wait("vector", cyo, (k.sc, k.sc.n))
                comp("vector", lambda v, c=c: v.tensor_tensor(
                    out=yb[:, :], in0=xtm2[:, :],
                    in1=sview(fcdTm[:, c * NH:], (1, 32), (0, 64)),
                    op=ALU.mult))
                vf()
                cy1 = comp("vector", lambda v: v.tensor_add(
                    yb[:, :], yb[:, :], psY[:, :]))
                wait("tensor", cy1)
                wait("scalar", cy1)
                cy2 = comp("scalar", lambda s: s.activation(
                    out=msb[:, 0:DI], in_=yb[:, :], func=AF.Square,
                    accum_out=rcol[:, :]))
                wait("scalar", cy2)
                cy2b = comp("scalar", lambda s: s.activation(
                    out=rcol[:, :], in_=rcol[:, :], func=AF.Ln,
                    scale=1.0 / DI, bias=1e-5))
                wait("scalar", cy2b)
                cy3 = comp("scalar", lambda s: s.activation(
                    out=rcol[:, :], in_=rcol[:, :], func=AF.Exp, scale=-0.5))
                if c == NCH - 1:
                    tap("yb", yb[:, :])
                wait("vector", cy3, tz2)
                cy4 = comp("vector", lambda v: v.scalar_tensor_tensor(
                    out=ygated[:, :], in0=yb[:, :], scalar=rcol[:, :],
                    in1=zst2[:, :], op0=ALU.mult, op1=ALU.mult))
                wait("tensor", cy4)
                if c == NCH - 1:
                    tap("ygated", ygated[:, :])
                with _ps(nc, "ptpF", [128, 512], BF16) as ptpF:
                    for tt0 in range(0, 16, 4):
                        for k2 in range(4):
                            em = (lambda t, tt0=tt0, k2=k2: t.transpose(
                                ptpF[:, k2 * 128:(k2 + 1) * 128],
                                ygated[:, (tt0 + k2) * 128:(tt0 + k2 + 1) * 128],
                                k.identB[:, :]))
                            if k2 == 3:
                                cy5 = comp("tensor", em)
                            else:
                                raw("tensor", em)
                        wait("scalar", cy5)
                        cy6 = comp("scalar", lambda s, tt0=tt0: s.copy(
                            out=ygT[:, tt0 * 128:(tt0 + 4) * 128],
                            in_=ptpF[:, 0:512]))
                        wait("tensor", cy6)
                with _ps(nc, "psO", [128, DM], F32) as psO:
                    wait("tensor", (k.sc, k.sc.n))
                    for kk in range(16):
                        for nb in (0, 1):
                            em = (lambda t, kk=kk, nb=nb: t.matmul(
                                psO[:, nb * 512:(nb + 1) * 512],
                                ygT[:, kk * 128:(kk + 1) * 128],
                                k.wtoB[:, kk * DM + nb * 512:
                                       kk * DM + (nb + 1) * 512],
                                start=(kk == 0), stop=(kk == 15)))
                            if kk == 15 and nb == 1:
                                cy7 = comp("tensor", em)
                            else:
                                raw("tensor", em)
                    wait("vector", cy7, tx0)
                    cy8 = comp("vector", lambda v: v.tensor_add(
                        x1b[:, :], psO[:, :], x0t2[:, :]))
                    wait("tensor", cy8)
                if own:
                    wait("sync", cy8)
                    t1s = dma("sync", k.x1_sp[c * 128:(c + 1) * 128, :],
                              x1b[:, :], sem=k.sze)
                wait("vector", cy8)
                raw("vector", lambda v: v.bn_stats(out=stat2[:, 0:6],
                                                   in_=x1b[:, 0:512]))
                comp("vector", lambda v: v.bn_stats(out=stat2[:, 6:12],
                                                    in_=x1b[:, 512:1024]))
                vf()
                cl1 = comp("vector", lambda v: v.bn_aggr(out=mv2[:, :],
                                                         in_=stat2[:, :]))
                wait("scalar", cl1)
                cl1b = comp("scalar", lambda s: s.activation(
                    out=rcol[:, :], in_=mv2[:, 1:2], func=AF.Ln, bias=1e-5))
                wait("scalar", cl1b)
                cl2 = comp("scalar", lambda s: s.activation(
                    out=rcol[:, :], in_=rcol[:, :], func=AF.Exp, scale=-0.5))
                wait("vector", cl2)
                cl3 = comp("vector", lambda v: v.tensor_scalar(
                    out=uln2[:, :], in0=x1b[:, :], scalar1=mv2[:, 0:1],
                    scalar2=rcol[:, :], op0=ALU.subtract, op1=ALU.mult))
                wait("tensor", cl3)
                with _ps(nc, "ptpG", [128, 512], BF16) as ptpG, \
                     _ps(nc, "psLG", [128, NEXP], F32) as psLG:
                    for g4 in range(2):
                        for k2 in range(4):
                            em = (lambda t, g4=g4, k2=k2: t.transpose(
                                ptpG[:, k2 * 128:(k2 + 1) * 128],
                                uln2[:, (g4 * 4 + k2) * 128:
                                     (g4 * 4 + k2 + 1) * 128],
                                k.identB[:, :]))
                            if k2 == 3:
                                cl4 = comp("tensor", em)
                            else:
                                raw("tensor", em)
                        wait("scalar", cl4)
                        cl5 = comp("scalar", lambda s, g4=g4: s.copy(
                            out=u2T[:, g4 * 512:(g4 + 1) * 512],
                            in_=ptpG[:, 0:512]))
                        wait("tensor", cl5)
                    for kk in range(8):
                        raw("tensor", lambda t, kk=kk: t.matmul(
                            psLG[:, :], u2T[:, kk * 128:(kk + 1) * 128],
                            k.wrB[:, kk * NEXP:(kk + 1) * NEXP],
                            start=(kk == 0), stop=False))
                    cl6 = comp("tensor", lambda t: t.matmul(
                        psLG[:, :], k.onesrbB[:, :], k.brB[:, :],
                        start=False, stop=True))
                    wait("vector", cl6)
                    cl7 = comp("vector", lambda v: v.tensor_copy(
                        out=lgs[:, :], in_=psLG[:, :]))
                    wait("tensor", cl7)
                wait("vector", cl7)
                cr1 = comp("vector", lambda v: v.reduce_max(
                    out=ngm[:, :], in_=lgs[:, :], axis=AX.X, negate=True))
                wait("scalar", cr1)
                cr2 = comp("scalar", lambda s: s.activation(
                    out=edump[:, :], in_=lgs[:, :], func=AF.Exp,
                    bias=ngm[:, :], accum_out=sexp[:, :]))
                wait("vector", cr2)
                raw("vector", lambda v: v.reciprocal(gcol[:, :], sexp[:, :]))
                comp("vector", lambda v: v.max(out=mx8[:, :], in_=lgs[:, :]))
                vf()
                cr3 = comp("vector", lambda v: v.max_index(
                    out=mi8[:, :], in_max=mx8[:, :], in_values=lgs[:, :]))
                wait("vector", cr3)
                cr4 = comp("vector", lambda v: v.tensor_copy(
                    out=idxf[:, :], in_=mi8[:, 0:1]))
                wait("vector", cr4)
                cr5 = comp("vector", lambda v: v.tensor_scalar(
                    out=mskf[:, :], in0=k.iota8B[:, :], scalar1=idxf[:, :],
                    scalar2=None, op0=ALU.is_equal))
                wait("vector", cr5)
                cr6 = comp("vector", lambda v: v.tensor_copy(
                    out=mskb16[:, :], in_=mskf[:, :]))
                wait("tensor", cr6)
                if own:
                    with _ps(nc, "psP", [128, NEXP], F32) as psP:
                        raw("tensor", lambda t: t.matmul(
                            psP[:, :], k.onesrfB[0:1, 0:128], prefrow[:, :],
                            start=True, stop=False))
                        cp1 = comp("tensor", lambda t: t.matmul(
                            psP[:, :], k.triliB[:, :], mskb16[:, :],
                            start=False, stop=True))
                        wait("vector", cp1)
                        comp("vector", lambda v: v.tensor_tensor(
                            out=posb[:, :], in0=psP[:, :], in1=mskf[:, :],
                            op=ALU.mult))
                        vf()
                        cp2 = comp("vector", lambda v, c=c: v.reduce_sum(
                            out=lpcols[:, c:c + 1], in_=posb[:, :], axis=AX.X))
                        wait("tensor", cp2)
                    raw("vector", lambda v, c=c: v.tensor_copy(
                        out=gcols[:, c:c + 1], in_=gcol[:, :]))
                    cp3 = comp("vector", lambda v, c=c: v.tensor_copy(
                        out=icols[:, c:c + 1], in_=idxf[:, :]))
                    wait("vector", cl3)
                    cp4 = comp("vector", lambda v: v.tensor_scalar(
                        out=poolr[:, :], in0=uln2[:, :], scalar1=gcol[:, :],
                        scalar2=None, op0=ALU.mult))
                    wait("sync", cp4)
                    tpl = dma("sync", k.pool[c * 128:(c + 1) * 128, :],
                              poolr[:, :], sem=k.sze)
                    wait("vector", tpl)
                cc1 = comp("tensor", lambda t, c=c: t.matmul(
                    psCnt[:, :], k.onescbB[:, :], mskb16[:, :],
                    start=True, stop=True))
                wait("vector", cc1)
                vf()
                cc2 = comp("vector", lambda v: v.tensor_tensor(
                    out=prefrow[:, :], in0=prefrow[:, :], in1=psCnt[:, :],
                    op=ALU.add))
                wait("tensor", cc2)

            vf()
            cf1 = comp("vector", lambda v: v.tensor_copy(
                out=ctmp[:, :], in_=prefrow[:, :]))
            wait("sync", cf1)
            tcf = dma("sync", k.cntsB[job:job + 1, :], ctmp[:, :],
                      sem=k.sze)
            wait("vector", tcf)
            tap("x1b", x1b[:, :])
            tap("lgs", lgs[:, :])
            tap("lpcols", lpcols[:, :])
            tap("gcols", gcols[:, :])
            tap("icols", icols[:, :])
        k.barrier()
        SD.close()


def rendezvous(k):
    nc, e = k.nc, k.ctx.enter_context
    wait, dma, comp, raw = k.wait, k.dma, k.comp, k.raw
    k.ctabB = e(_sb(nc, "ctabB", [NEXP, NEXP], F32))
    cz = comp("vector", lambda v: v.memset(k.ctabB[:, :], 0.0))
    k.rdvscr = e(_sb(nc, "rdvscr", [1, 16], I32))
    arow = e(_sb(nc, "arow", [2, 16], I32))
    brow = e(_sb(nc, "brow", [2, 16], I32))
    crow = e(_sb(nc, "crow", [2, 16], I32))
    frow = e(_sb(nc, "frow", [2, 16], I32))

    MAGICA = MAGIC
    MAGICB = MAGIC + 1
    c1 = comp("vector", lambda v: v.memset(arow[:, :], 0))
    k.vf()
    c2 = comp("vector", lambda v: v.memset(arow[:, 0:1], MAGICA))
    comp("vector", lambda v: v.memset(brow[:, :], 0))
    k.vf()
    comp("vector", lambda v: v.memset(brow[:, 0:1], MAGICA))
    k.vf()
    c3 = comp("vector", lambda v: v.memset(brow[:, 1:2], MAGICB))
    comp("vector", lambda v: v.memset(crow[:, :], 0))
    k.vf()
    c4 = comp("vector", lambda v: v.memset(crow[:, 1:2], MAGICB))
    wait("gpsimd", c1, c2, c3, c4, cz, (k.sze, k.sze.n))
    sgh = k.sg
    base = sgh.n

    bcf = nc.gpsimd.alloc_register("bcflag")
    bcc = nc.gpsimd.alloc_register("bccnt")

    def poll(g, nd, rc, f0, f1, f2, col, val):
        g.reg_mov(nd, 1)
        with g.While(nd):
            g.reg_add(rc, rc, 16)
            g.dma_start(out=frow[:, :],
                        in_=k.sh_flag[0:2, :]).then_inc(sgh.h, 16)
            g.wait_ge(sgh.h, rc)
            g.load(f0, frow[0:1, col:col + 1])
            g.load(f1, frow[1:2, col:col + 1])
            g.reg_alu(f0, f0, val, ALU.is_equal)
            g.reg_alu(f1, f1, val, ALU.is_equal)
            g.reg_alu(f2, f0, f1, ALU.bitwise_and)
            g.reg_alu(nd, f2, 1, ALU.is_lt)

    def body(g):
        g.reg_mov(bcf, 2)
        g.reg_mov(bcc, NEXP + 2)
        with (g.register("nd") as nd, g.register("rc") as rc,
              g.register("f0") as f0, g.register("f1") as f1,
              g.register("f2") as f2):
            g.reg_mov(rc, base)
            g.dma_start(out=k.sh_cnt[0:NEXP, :],
                        in_=k.zer88[:, :]).then_inc(sgh.h, 16)
            g.dma_start(out=k.sh_cnt[NEXP:NEXP + 3, :],
                        in_=k.zer88[0:3, :]).then_inc(sgh.h, 16)
            g.reg_add(rc, rc, 32)
            g.wait_ge(sgh.h, rc)
            g.indirect_dma_start(
                out=k.sh_flag,
                out_offset=IndirectOffsetOnAxis(k.parrB[:, 0:1], 0),
                in_=arow[:, :], in_offset=None,
                bounds_check=bcf, oob_is_err=False).then_inc(sgh.h, 16)
            g.reg_add(rc, rc, 16)
            g.wait_ge(sgh.h, rc)
            poll(g, nd, rc, f0, f1, f2, 0, MAGICA)
            g.indirect_dma_start(
                out=k.sh_cnt,
                out_offset=IndirectOffsetOnAxis(k.jobrB[:, 0:1], 0),
                in_=k.cntsB[:, :], in_offset=None,
                bounds_check=bcc, oob_is_err=False).then_inc(sgh.h, 16)
            g.reg_add(rc, rc, 16)
            g.wait_ge(sgh.h, rc)
            g.indirect_dma_start(
                out=k.sh_flag,
                out_offset=IndirectOffsetOnAxis(k.parrB[:, 0:1], 0),
                in_=brow[:, :], in_offset=None,
                bounds_check=bcf, oob_is_err=False).then_inc(sgh.h, 16)
            g.reg_add(rc, rc, 16)
            g.wait_ge(sgh.h, rc)
            poll(g, nd, rc, f0, f1, f2, 1, MAGICB)
            g.indirect_dma_start(
                out=k.ctabB[:, :], out_offset=None,
                in_=k.sh_cnt,
                in_offset=IndirectOffsetOnAxis(k.vrowsB[:, 0:1], 0),
                bounds_check=bcc, oob_is_err=False).then_inc(sgh.h, 16)
            g.reg_add(rc, rc, 16)
            g.wait_ge(sgh.h, rc)
            g.indirect_dma_start(
                out=k.sh_flag,
                out_offset=IndirectOffsetOnAxis(k.parrB[:, 0:1], 0),
                in_=crow[:, :], in_offset=None,
                bounds_check=bcf, oob_is_err=False).then_inc(sgh.h, 16)
            g.reg_add(rc, rc, 16)
            g.wait_ge(sgh.h, rc)
    if RDV_LOCAL:
        def body_local(g):
            g.dma_start(out=k.ctabB[0:NJOBS, :],
                        in_=k.cntsB[:, :]).then_inc(sgh.h, 16)
            g.wait_ge(sgh.h, base + 16)
        k.P.steps.append(("gpsimd", body_local))
    else:
        k.P.steps.append(("gpsimd", body))
    sgh.n = base + 128
    cq = comp("gpsimd", lambda g: g.memset(k.rdvscr[:, :], 0))
    for en in ("sync", "tensor", "vector", "scalar"):
        wait(en, cq)


def ffn(k):
    nc, e = k.nc, k.ctx.enter_context
    wait, dma, comp, raw, tap = k.wait, k.dma, k.comp, k.raw, k.tap
    vf = k.vf

    bcregs = {}
    for bval in (7, NTIL * 128 - 1, NEXP * DM - 1):
        bcregs[bval] = k.nc.gpsimd.alloc_register(f"bc{bval}")

    def _initbc(g, bcregs=bcregs):
        for bval, reg in bcregs.items():
            g.reg_mov(reg, bval)
    k.P.steps.append(("gpsimd", _initbc))

    def idma(out, out_off, in_, in_off, bc):
        s = k.sg2
        s.n += 16
        reg = bcregs[bc]

        def fn(g, out=out, out_off=out_off, in_=in_, in_off=in_off, reg=reg):
            g.indirect_dma_start(
                out=out, out_offset=out_off, in_=in_, in_offset=in_off,
                bounds_check=reg, oob_is_err=False).then_inc(s.h, 16)
        k.P.steps.append(("gpsimd", fn))
        return (s, s.n)

    with ExitStack() as S:
        es = S.enter_context
        offsr = es(_sb(nc, "offsr", [1, NEXP], F32))
        mycnt = es(_sb(nc, "mycnt", [1, NEXP], F32))
        pcr = es(_sb(nc, "pcr", [1, NEXP], F32))
        pci = es(_sb(nc, "pci", [1, NEXP], I32))
        inclr = es(_sb(nc, "inclr", [1, NEXP], F32))
        plocr = es(_sb(nc, "plocr", [1, NEXP], F32))
        zer8 = es(_sb(nc, "zer8", [1, NEXP], F32))
        inclc = es(_sb(nc, "inclc", [NEXP, 1], F32))
        cmp8 = es(_sb(nc, "cmp8", [NEXP, NTIL], F32))
        cmpb = es(_sb(nc, "cmpb", [NEXP, NTIL], BF16))
        etbl = es(_sb(nc, "etbl", [1, NTIL], F32))
        etblp = es(_sb(nc, "etblp", [1, NTIL + 1], F32))
        etadj = es(_sb(nc, "etadj", [1, NTIL], F32))
        etadjB = es(_sb(nc, "etadjB", [128, NTIL], F32))
        ebadj = es(_sb(nc, "ebadj", [1, NTIL], F32))
        flg = es(_sb(nc, "flg", [1, NTIL], F32))
        plocB = es(_sb(nc, "plocB", [128, NEXP], F32))
        offsB = es(_sb(nc, "offsB", [128, NEXP], F32))
        mskf2 = es(_sb(nc, "mskf2", [128, NEXP], F32))
        tcol = es(_sb(nc, "tcol", [128, 1], F32))
        tcol2 = es(_sb(nc, "tcol2", [128, 1], F32))
        dcols = es(_sb(nc, "dcols", [128, NCH], I32))
        gkcols = es(_sb(nc, "gkcols", [128, NCH], F32))
        poolc = es(_sb(nc, "poolc", [128, DM], BF16))
        bidx = es(_sb(nc, "bidx", [2, 1], I32))
        wi = es(_sb(nc, "wi", [128, 1], I32))
        w1S = es(_sb(nc, "w1S", [128, 8 * DM], BF16))
        w2S = es(_sb(nc, "w2S", [128, 8 * DM], BF16))
        bd1h = es(_sb(nc, "bd1h", [2, DM], BF16))
        bd1l = es(_sb(nc, "bd1l", [2, DM], BF16))
        bd2h = es(_sb(nc, "bd2h", [2, DM], BF16))
        bd2l = es(_sb(nc, "bd2l", [2, DM], BF16))
        srows = es(_sb(nc, "srows", [128, DM], BF16))
        xsT = es(_sb(nc, "xsT", [128, 8 * 128], BF16))
        hfm = es(_sb(nc, "hfm", [128, 8 * 128], BF16))
        eot = es(_sb(nc, "eot", [128, DM], F32))
        eog = es(_sb(nc, "eog", [128, DM], F32))
        x1r = es(_sb(nc, "x1r", [128, DM], F32))
        outr = es(_sb(nc, "outr", [128, DM], F32))

        with _ps(nc, "psF", [128, 64], F32) as psF:
            c0 = comp("vector", lambda v: v.memset(zer8[:, :], 0.0))
            cmy = comp("vector", lambda v: v.tensor_copy(
                out=mycnt[:, :], in_=k.cntsB[0:1, :]))
            wait("vector", (k.sq, k.sq.n))
            csel = comp("vector", lambda v: v.select(
                out=k.ctabB[:, :], mask=k.validB[:, :],
                on_true=k.ctabB[:, :], on_false=k.zer88[:, :]))
            wait("tensor", (k.sv, k.sv.n), (k.sq, k.sq.n), csel)
            cf1 = comp("tensor", lambda t: t.matmul(
                psF[0:1, 0:NEXP], k.prefmB[:, :], k.ctabB[:, :],
                start=True, stop=True))
            wait("vector", cf1)
            cf2 = comp("vector", lambda v: v.tensor_copy(
                out=offsr[:, :], in_=psF[0:1, 0:NEXP]))
            wait("tensor", cf2)
            wait("vector", cmy)
            comp("vector", lambda v: v.tensor_scalar(
                out=pcr[:, :], in0=mycnt[:, :], scalar1=63.5,
                scalar2=1.0 / 128.0, op0=ALU.add, op1=ALU.mult))
            vf()
            c1 = comp("vector", lambda v: v.tensor_copy(out=pci[:, :],
                                                        in_=pcr[:, :]))
            wait("vector", c1)
            comp("vector", lambda v: v.tensor_copy(out=pcr[:, :],
                                                   in_=pci[:, :]))
            vf()
            c2 = comp("vector", lambda v: v.tensor_scalar_mul(
                pcr[:, :], pcr[:, :], 128.0))
            wait("vector", c2, c0)
            c3 = comp("vector", lambda v: v.tensor_tensor_scan(
                out=inclr[:, :], data0=pcr[:, :], data1=zer8[:, :],
                initial=0.0, op0=ALU.add, op1=ALU.add))
            wait("vector", c3)
            c4 = comp("vector", lambda v: v.tensor_tensor(
                out=plocr[:, :], in0=inclr[:, :], in1=pcr[:, :],
                op=ALU.subtract))
            wait("sync", c3)
            t1 = dma("sync", inclc[:, :], inclr[:, :], sem=k.sze)
            wait("vector", t1)
            c5 = comp("vector", lambda v: v.tensor_scalar(
                out=cmp8[:, :], in0=k.tilestB[:, :], scalar1=inclc[:, :],
                scalar2=None, op0=ALU.is_ge))
            wait("vector", c5)
            c6 = comp("vector", lambda v: v.tensor_copy(out=cmpb[:, :],
                                                        in_=cmp8[:, :]))
            wait("tensor", c6)
            cf3 = comp("tensor", lambda t: t.matmul(
                psF[0:1, 32:32 + NTIL], k.onescbB[0:NEXP, :], cmpb[:, :],
                start=True, stop=True))
            wait("vector", cf3)
            c7 = comp("vector", lambda v: v.tensor_copy(
                out=etbl[:, :], in_=psF[0:1, 32:32 + NTIL]))
            wait("vector", c7)
            raw("vector", lambda v: v.memset(etblp[:, 0:1], -1.0))
            c8 = comp("vector", lambda v: v.tensor_copy(
                out=etblp[:, 1:NTIL + 1], in_=etbl[:, :]))
            wait("vector", c8)
            c9 = comp("vector", lambda v: v.tensor_tensor(
                out=flg[:, :], in0=etbl[:, :], in1=etblp[:, 0:NTIL],
                op=ALU.is_equal))
            wait("vector", c9)
            raw("vector", lambda v: v.tensor_scalar(
                out=etadj[:, :], in0=flg[:, :], scalar1=float(1 << 27),
                scalar2=None, op0=ALU.mult))
            comp("vector", lambda v: v.tensor_scalar(
                out=ebadj[:, :], in0=etbl[:, :], scalar1=1024.0,
                scalar2=None, op0=ALU.mult))
            vf()
            c11 = comp("vector", lambda v: v.tensor_tensor(
                out=etadj[:, :], in0=etadj[:, :], in1=ebadj[:, :],
                op=ALU.add))
            c12 = comp("vector", lambda v: v.tensor_copy(out=ebadj[:, :],
                                                         in_=etbl[:, :]))
            wait("tensor", c4, cf2, c11, c12)
            cf4 = comp("tensor", lambda t: t.matmul(
                psF[:, 32:32 + NEXP], k.onesrfB[0:1, 0:128], plocr[:, :],
                start=True, stop=True))
            wait("vector", cf4)
            cb1 = comp("vector", lambda v: v.tensor_copy(
                out=plocB[:, :], in_=psF[:, 32:32 + NEXP]))
            wait("tensor", cb1)
            cf5 = comp("tensor", lambda t: t.matmul(
                psF[:, 32:32 + NEXP], k.onesrfB[0:1, 0:128], offsr[:, :],
                start=True, stop=True))
            wait("vector", cf5)
            cb2 = comp("vector", lambda v: v.tensor_copy(
                out=offsB[:, :], in_=psF[:, 32:32 + NEXP]))
            wait("tensor", cb2)
            cf6 = comp("tensor", lambda t: t.matmul(
                psF[:, 32:32 + NTIL], k.onesrfB[0:1, 0:128], etadj[:, :],
                start=True, stop=True))
            wait("vector", cf6)
            cb3 = comp("vector", lambda v: v.tensor_copy(
                out=etadjB[:, :], in_=psF[:, 32:32 + NTIL]))
            wait("tensor", cb3)

        tap("etbl", etbl[:, :])
        tap("plocr", plocr[:, :])
        tap("offsr", offsr[:, :])
        tap("ctab", k.ctabB[:, :])
        tap("mycnt", mycnt[:, :])

        pscr = es(_sb(nc, "pscr", [128, NEXP], F32))
        wic = es(_sb(nc, "wic", [128, 8], I32))
        wicf = es(_sb(nc, "wicf", [128, 8], F32))
        for c in range(NCH):
            cq1 = comp("vector", lambda v, c=c: v.tensor_scalar(
                out=mskf2[:, :], in0=k.iota8B[:, :],
                scalar1=k.icols[:, c:c + 1], scalar2=None, op0=ALU.is_equal))
            wait("vector", cq1)
            comp("vector", lambda v: v.tensor_tensor(
                out=pscr[:, :], in0=mskf2[:, :], in1=plocB[:, :],
                op=ALU.mult))
            vf()
            comp("vector", lambda v: v.reduce_sum(
                out=tcol[:, :], in_=pscr[:, :], axis=AX.X))
            vf()
            comp("vector", lambda v, c=c: v.tensor_scalar(
                out=tcol[:, :], in0=tcol[:, :],
                scalar1=k.lpcols[:, c:c + 1], scalar2=-1.0,
                op0=ALU.add, op1=ALU.add))
            vf()
            cq2 = comp("vector", lambda v, c=c: v.tensor_copy(
                out=dcols[:, c:c + 1], in_=tcol[:, :]))
            vf()
            comp("vector", lambda v: v.tensor_tensor(
                out=pscr[:, :], in0=mskf2[:, :], in1=offsB[:, :],
                op=ALU.mult))
            vf()
            comp("vector", lambda v: v.reduce_sum(
                out=tcol2[:, :], in_=pscr[:, :], axis=AX.X))
            vf()
            comp("vector", lambda v, c=c: v.tensor_scalar(
                out=tcol2[:, :], in0=tcol2[:, :],
                scalar1=k.lpcols[:, c:c + 1], scalar2=None, op0=ALU.add))
            vf()
            comp("vector", lambda v: v.tensor_scalar(
                out=tcol2[:, :], in0=tcol2[:, :], scalar1=float(CAP),
                scalar2=None, op0=ALU.is_lt))
            vf()
            cq3 = comp("vector", lambda v, c=c: v.tensor_scalar(
                out=gkcols[:, c:c + 1], in0=tcol2[:, :],
                scalar1=k.gcols[:, c:c + 1], scalar2=None, op0=ALU.mult))
            tp1 = dma("sync", poolc[:, :], k.pool[c * 128:(c + 1) * 128, :],
                      sem=k.sze)
            wait("gpsimd", cq2, cq3, tp1)
            ts1 = idma(k.sorted_x, IndirectOffsetOnAxis(dcols[:, c:c + 1], 0),
                       poolc[:, :], None, NTIL * 128 - 1)
            wait("sync", ts1)

        tap("dcols", dcols[:, :])
        tap("gkcols", gkcols[:, :])

        cbi = comp("vector", lambda v: v.memset(bidx[:, :], 100))
        with _ps(nc, "psH", [128, 8 * 128], F32) as psH, \
             _ps(nc, "psO2", [128, DM], F32) as psO2, \
             _ps(nc, "ptpH", [128, 512], BF16) as ptpH:
            for j in range(NTIL):
                wait("vector", (k.sg2, k.sg2.n), (k.sp, k.sp.n))
                comp("vector", lambda v, j=j: v.tensor_single_scalar(
                    out=wicf[:, :], in_=k.iotakB[:, :],
                    scalar=etadjB[:, j:j + 1], op=ALU.add))
                vf()
                raw("vector", lambda v: v.tensor_copy(
                    out=wic[:, :], in_=wicf[:, :]))
                cw1 = comp("vector", lambda v, j=j: v.tensor_copy(
                    out=bidx[0:1, 0:1], in_=ebadj[0:1, j:j + 1]))
                wait("gpsimd", cw1, (k.sp, k.sp.n))
                gw = []
                for kc in range(8):
                    gw.append(idma(w1S[:, kc * DM:(kc + 1) * DM], None,
                                   k.w1bf,
                                   IndirectOffsetOnAxis(wic[:, kc:kc + 1], 0),
                                   NEXP * DM - 1))
                    gw.append(idma(w2S[:, kc * DM:(kc + 1) * DM], None,
                                   k.w2bf,
                                   IndirectOffsetOnAxis(wic[:, kc:kc + 1], 0),
                                   NEXP * DM - 1))
                gw.append(idma(bd1h[:, :], None, k.b1hi,
                               IndirectOffsetOnAxis(bidx[:, 0:1], 0), 7))
                gw.append(idma(bd1l[:, :], None, k.b1lo,
                               IndirectOffsetOnAxis(bidx[:, 0:1], 0), 7))
                gw.append(idma(bd2h[:, :], None, k.b2hi,
                               IndirectOffsetOnAxis(bidx[:, 0:1], 0), 7))
                gw.append(idma(bd2l[:, :], None, k.b2lo,
                               IndirectOffsetOnAxis(bidx[:, 0:1], 0), 7))
                wait("sync", (k.sp, k.sp.n), (k.sg2, k.sg2.n))
                tr1 = dma("sync", srows[:, :],
                          k.sorted_x[j * 128:(j + 1) * 128, :], sem=k.sze)
                wait("tensor", tr1)
                for g4 in range(2):
                    for k2 in range(4):
                        em = (lambda t, g4=g4, k2=k2: t.transpose(
                            ptpH[:, k2 * 128:(k2 + 1) * 128],
                            srows[:, (g4 * 4 + k2) * 128:
                                  (g4 * 4 + k2 + 1) * 128],
                            k.identB[:, :]))
                        if k2 == 3:
                            cs1 = comp("tensor", em)
                        else:
                            raw("tensor", em)
                    wait("scalar", cs1)
                    cs2 = comp("scalar", lambda s, g4=g4: s.copy(
                        out=xsT[:, g4 * 512:(g4 + 1) * 512],
                        in_=ptpH[:, 0:512]))
                    wait("tensor", cs2)
                wait("tensor", *gw)
                for ft in range(8):
                    for kc in range(8):
                        raw("tensor", lambda t, ft=ft, kc=kc: t.matmul(
                            psH[:, ft * 128:(ft + 1) * 128],
                            w1S[:, kc * DM + ft * 128:kc * DM + (ft + 1) * 128],
                            xsT[:, kc * 128:(kc + 1) * 128],
                            start=(kc == 0), stop=False))
                    raw("tensor", lambda t, ft=ft: t.matmul(
                        psH[:, ft * 128:(ft + 1) * 128],
                        bd1h[0:1, ft * 128:(ft + 1) * 128],
                        k.onesrbB[:, :], start=False, stop=False))
                    em = (lambda t, ft=ft: t.matmul(
                        psH[:, ft * 128:(ft + 1) * 128],
                        bd1l[0:1, ft * 128:(ft + 1) * 128],
                        k.onesrbB[:, :], start=False, stop=True))
                    if ft == 7:
                        cm1 = comp("tensor", em)
                    else:
                        raw("tensor", em)
                wait("scalar", cm1)
                cm2 = comp("scalar", lambda s: s.activation(
                    out=hfm[:, :], in_=psH[:, :], func=AF.Relu))
                wait("tensor", cm2)
                for kf in range(8):
                    for nb in (0, 1):
                        raw("tensor", lambda t, kf=kf, nb=nb: t.matmul(
                            psO2[:, nb * 512:(nb + 1) * 512],
                            hfm[:, kf * 128:(kf + 1) * 128],
                            w2S[:, kf * DM + nb * 512:
                                kf * DM + (nb + 1) * 512],
                            start=(kf == 0), stop=False))
                for nb in (0, 1):
                    raw("tensor", lambda t, nb=nb: t.matmul(
                        psO2[:, nb * 512:(nb + 1) * 512], k.onesrbB[:, :],
                        bd2h[0:1, nb * 512:(nb + 1) * 512],
                        start=False, stop=False))
                cm3 = None
                for nb in (0, 1):
                    cm3 = comp("tensor", lambda t, nb=nb: t.matmul(
                        psO2[:, nb * 512:(nb + 1) * 512], k.onesrbB[:, :],
                        bd2l[0:1, nb * 512:(nb + 1) * 512],
                        start=False, stop=True))
                wait("vector", cm3)
                cm4 = comp("vector", lambda v: v.tensor_copy(
                    out=eot[:, :], in_=psO2[:, :]))
                wait("tensor", cm4)
                wait("sync", cm4)
                te1 = dma("sync", k.eo_sp[j * 128:(j + 1) * 128, :],
                          eot[:, :], sem=k.sze)
                wait("vector", te1)

        for c in range(NCH):
            wait("gpsimd", (k.sv, k.sv.n), (k.sze, k.sze.n))
            tg1 = idma(eog[:, :], None, k.eo_sp,
                       IndirectOffsetOnAxis(dcols[:, c:c + 1], 0),
                       NTIL * 128 - 1)
            tx1 = dma("sync", x1r[:, :], k.x1_sp[c * 128:(c + 1) * 128, :],
                      sem=k.sze)
            wait("vector", tg1, tx1)
            co1 = comp("vector", lambda v, c=c: v.scalar_tensor_tensor(
                out=outr[:, :], in0=eog[:, :],
                scalar=gkcols[:, c:c + 1], in1=x1r[:, :],
                op0=ALU.mult, op1=ALU.add))
            wait("sync", co1)
            to1 = dma("sync", k.x2_out[c * 128:(c + 1) * 128, :], outr[:, :],
                      sem=k.sze)
            wait("sync", to1)
            wait("vector", to1)
        k.barrier()


def build_full():
    k = _mk()
    prep(k)
    if STOP_AFTER != "prep":
        for job in range(EMIT_JOBS):
            attn_job(k, job)
        if STOP_AFTER is None:
            rendezvous(k)
            if EMIT_FFN:
                ffn(k)
    k.barrier()
    with k.ctx:
        with k.nc.Block() as blk:
            k.P.build(blk)
    return k


def host_inputs(inputs, core):
    f32 = np.float32
    u = np.ascontiguousarray(np.asarray(inputs["L0_attn_u"], f32))
    jobs = [core - 2 * i for i in range(NJOBS)]
    jobs = [b if b >= 0 else jobs[0] for b in jobs]
    seen = set()
    rows = np.zeros((NJOBS, 1), np.int32)
    for s, b in enumerate(jobs):
        if b in seen:
            rows[s] = NEXP + (s - 1)
        else:
            rows[s] = b
            seen.add(b)
    bf = ml_dtypes.bfloat16
    ident = np.eye(128, dtype=bf)
    sidx = np.arange(128)
    maskf = (sidx[:, None] < sidx[None, :]).astype(bf)
    maskb = (sidx[:, None] > sidx[None, :]).astype(bf)
    trili = (sidx[:, None] <= sidx[None, :]).astype(bf)
    sel3 = np.zeros((96, 4096), bf)
    for r in range(96):
        h = r % 32
        sel3[r, h * 128:(h + 1) * 128] = 1
    sel2h4 = np.zeros((4, 128), bf)
    sel2h4[0, 0:64] = 1
    sel2h4[1, 64:128] = 1
    sel2h4[2, 0:64] = 1
    sel2h4[3, 64:128] = 1
    conv_w = np.asarray(inputs["L0_attn_conv_w"], f32)
    conv_wc = np.zeros((128, 126), f32)
    for j in range(18):
        for t in range(7):
            conv_wc[:, j * 7 + t] = conv_w[j * 128:(j + 1) * 128, 0, t]
    conv_bc = np.asarray(inputs["L0_attn_conv_b"], f32).reshape(18, 128).T.copy()
    dtb = np.asarray(inputs["L0_attn_dt_bias"], f32)
    alog = np.asarray(inputs["L0_attn_A_log"], f32)
    W1 = np.asarray(inputs["L0_ffn_W1"], f32)
    W2 = np.asarray(inputs["L0_ffn_W2"], f32)
    return {
        "u_jobs": np.concatenate([u[b] for b in jobs], 0),
        "wt_in": np.ascontiguousarray(
            np.asarray(inputs["L0_attn_in_proj_w"], f32).T),
        "wt_out": np.ascontiguousarray(
            np.asarray(inputs["L0_attn_out_proj_w"], f32).T),
        "fcdt": np.ascontiguousarray(np.asarray(inputs["L0_attn_fc_D_w"], f32).T),
        "conv_wc": conv_wc,
        "conv_bc": np.ascontiguousarray(conv_bc),
        "dtb2": np.concatenate([dtb, dtb]).reshape(64, 1),
        "alog2": np.concatenate([alog, alog]).reshape(64, 1),
        "dparr": np.asarray(inputs["L0_attn_D_param"], f32).reshape(1, NH),
        "nwc": np.asarray(inputs["L0_attn_norm_w"], f32).reshape(16, 128).T.copy(),
        "wr": np.asarray(inputs["L0_ffn_Wr"], f32),
        "br": np.asarray(inputs["L0_ffn_br"], f32).reshape(1, NEXP),
        "w1f": W1.reshape(NEXP * DM, DM),
        "w2f": W2.reshape(NEXP * DM, DM),
        "b1r": np.asarray(inputs["L0_ffn_b1"], f32),
        "b2r": np.asarray(inputs["L0_ffn_b2"], f32),
        "identb": ident,
        "identf": np.eye(128, dtype=f32),
        "maskf": maskf,
        "maskb": maskb,
        "trili": trili,
        "sel3": sel3,
        "sel2h4": sel2h4,
        "iota8r": np.broadcast_to(np.arange(8, dtype=f32), (128, 8)).copy(),
        "iotak": (np.arange(128)[:, None] +
                  128 * np.arange(8)[None, :]).astype(f32),
        "tilest": np.broadcast_to(128.0 * np.arange(NTIL, dtype=f32),
                                  (NEXP, NTIL)).copy(),
        "onescb": np.ones((128, 1), bf),
        "onesrb": np.ones((1, 128), bf),
        "ones2b": np.ones((2, 128), bf),
        "onesrf": np.ones((1, 512), f32),
        "prefm": (np.arange(8) < core).astype(f32).reshape(NEXP, 1),
        "validm": np.broadcast_to(
            (np.arange(8) < 2 * (core // 2) + 2).astype(np.int32)[:, None],
            (NEXP, NEXP)).copy(),
        "jobrows": rows,
        "parrow": np.array([[core % 2], [2]], np.int32),
        "sorted_x": np.zeros((NTIL * 128, DM), bf),
        "vrows": np.where(np.arange(8) < 2 * (core // 2) + 2,
                          np.arange(8), 1000).astype(np.int32).reshape(8, 1),
        "cvals": np.broadcast_to(
            np.array([1e-5, 1.0, -1.0, -0.5, 1.0 / 2048.0, 0.0, 0.0, 0.0],
                     f32), (128, 8)).copy(),
    }


def kernel(**inputs):
    k = build_full()
    in_maps = [host_inputs(inputs, c) for c in range(8)]
    res = run_bass_kernel_spmd(k.nc, in_maps, core_ids=list(range(8)))
    out = np.stack([np.asarray(res.results[c]["x2"]) for c in range(8)], 0)
    return out.astype(np.float32)


# version 04 (final)
# speedup vs baseline: 1.0013x; 1.0013x over previous
import numpy as np
import ml_dtypes
import concourse.bass as bass
import concourse.mybir as mybir
from concourse.bass import IndirectOffsetOnAxis
from concourse.bass_utils import run_bass_kernel_spmd
from contextlib import ExitStack

F32 = mybir.dt.float32
BF16 = mybir.dt.bfloat16
I32 = mybir.dt.int32
U32 = mybir.dt.uint32
AF = mybir.ActivationFunctionType
ALU = mybir.AluOpType
AX = mybir.AxisListType

L = 2048
DM = 1024
DI = 2048
NH = 32
HP = 64
QC = 128
NCH = L // QC
FIP = 4416
NJOBS = 4
NEXP = 8
CAP = 2048
NTIL = 24
MAGIC = 1234567

DEBUG_TAPS = []
EMIT_JOBS = NJOBS
EMIT_FFN = True
STOP_AFTER = None
RDV_LOCAL = False




def _spackap(tensor_ap, c, h_lo):
    return bass.AP(tensor_ap.tensor, c * 64 * 2048 + h_lo * 64,
                   [[2048, 64], [128, 16], [1, 64]])


def _sb(nc, name, shape, dtype):
    return nc.sbuf_tensor(name, shape, dtype, allow_name_mangling=True)


def _ps(nc, name, shape, dtype):
    return nc.psum_tensor(name, shape, dtype, allow_name_mangling=True)


def sview(base_ap, *dims):
    part = list(base_ap.ap[0])
    return bass.AP(base_ap.tensor, base_ap.offset,
                   [part] + [[int(s), int(c)] for s, c in dims])


class Prog:
    def __init__(self):
        self.steps = []

    def build(self, blk):
        for name in ("sync", "tensor", "vector", "scalar", "gpsimd"):
            steps = [fn for en, fn in self.steps if en == name]
            if not steps:
                continue

            def body(eng, steps=steps):
                for fn in steps:
                    fn(eng)
            getattr(blk, name)(body)


class K:
    pass


def _mk():
    k = K()
    nc = bass.Bass("TRN2", target_bir_lowering=False,
                   detect_race_conditions=False)
    k.nc = nc
    P = Prog()
    k.P = P
    k.in_names = []

    def d(name, shape, dtype=F32):
        k.in_names.append(name)
        return nc.dram_tensor(name, list(shape), dtype,
                              kind="ExternalInput").ap()

    k.u_jobs = d("u_jobs", [NJOBS * L, DM])
    k.wt_in = d("wt_in", [DM, FIP])
    k.wt_out = d("wt_out", [DI, DM])
    k.fcdt = d("fcdt", [DI, NH])
    k.conv_wc = d("conv_wc", [128, 18 * 7])
    k.conv_bc = d("conv_bc", [128, 18])
    k.dtb2 = d("dtb2", [64, 1])
    k.alog2 = d("alog2", [64, 1])
    k.dparr = d("dparr", [1, NH])
    k.nwc = d("nwc", [128, 16])
    k.wr = d("wr", [DM, NEXP])
    k.br = d("br", [1, NEXP])
    k.w1f = d("w1f", [NEXP * DM, DM])
    k.w2f = d("w2f", [NEXP * DM, DM])
    k.b1r = d("b1r", [NEXP, DM])
    k.b2r = d("b2r", [NEXP, DM])
    k.identb = d("identb", [128, 128], BF16)
    k.identf = d("identf", [128, 128])
    k.maskf = d("maskf", [128, 128], BF16)
    k.maskb = d("maskb", [128, 128], BF16)
    k.trili = d("trili", [128, 128], BF16)
    k.sel3 = d("sel3", [96, 4096], BF16)
    k.sel2h4 = d("sel2h4", [4, 128], BF16)
    k.iota8r = d("iota8r", [128, NEXP])
    k.iotak = d("iotak", [128, 8])
    k.tilest = d("tilest", [NEXP, NTIL])
    k.onescb = d("onescb", [128, 1], BF16)
    k.onesrb = d("onesrb", [1, 128], BF16)
    k.ones2b = d("ones2b", [2, 128], BF16)
    k.onesrf = d("onesrf", [1, 512])
    k.prefm = d("prefm", [NEXP, 1])
    k.validm = d("validm", [NEXP, NEXP], I32)
    k.vrows = d("vrows", [NEXP, 1], I32)
    k.jobrows = d("jobrows", [NJOBS, 1], I32)
    k.parrow = d("parrow", [2, 1], I32)
    k.cvals = d("cvals", [128, 8])

    k.x2_out = nc.dram_tensor("x2", [L, DM], F32, kind="ExternalOutput").ap()
    k.dbg = {}
    for name, shape, dt in DEBUG_TAPS:
        k.dbg[name] = nc.dram_tensor("dbg_" + name, list(shape), dt,
                                     kind="ExternalOutput").ap()

    def scr(name, shape, dtype):
        return nc.dram_tensor(name, list(shape), dtype, kind="Internal").ap()

    k.wtin_bf = scr("wtin_bf", [DM, FIP], BF16)
    k.w1bf = scr("w1bf", [NEXP * DM, DM], BF16)
    k.w2bf = scr("w2bf", [NEXP * DM, DM], BF16)
    k.b1hi = scr("b1hi", [NEXP, DM], BF16)
    k.b1lo = scr("b1lo", [NEXP, DM], BF16)
    k.b2hi = scr("b2hi", [NEXP, DM], BF16)
    k.b2lo = scr("b2lo", [NEXP, DM], BF16)
    k.x1_kind = "Internal"
    k.xt_sp = scr("xt_sp", [NCH, 16, 128, 128], BF16)
    k.xc_sp = scr("xc_sp", [16, 128, L], BF16)
    k.zs_sp = scr("zs_sp", [NCH, 128, DI], BF16)
    k.sfin_sp = scr("sfin_sp", [NCH, 64, 2048], BF16)
    k.sb_sp = scr("sb_sp", [NCH, 64, 2048], BF16)
    import os as _os
    k.x1_sp = nc.dram_tensor(
        "x1_sp", [L, DM], F32,
        kind=("ExternalOutput" if _os.environ.get("X1DEBUG")
              else "Internal")).ap()
    k.pool = scr("pool", [L, DM], BF16)
    k.tfl = scr("tfl", [192, L], BF16)
    k.efl = scr("efl", [64, 128], BF16)
    k.lfl = scr("lfl", [128, 16], BF16)
    k.in_names.append("sorted_x")
    k.sorted_x = nc.dram_tensor("sorted_x", [NTIL * 128, DM], BF16,
                                kind="ExternalInput").ap()
    k.eo_sp = nc.dram_tensor(
        "eo_sp", [NTIL * 128, DM], F32,
        kind=("ExternalOutput" if _os.environ.get("MOEDEBUG")
              else "Internal")).ap()
    k.sh_cnt = nc.dram_tensor("sh_cnt", [NEXP + 3, NEXP], F32,
                              addr_space="Shared", kind="Internal").ap()
    k.sh_flag = nc.dram_tensor("sh_flag", [3, 16], I32,
                               addr_space="Shared", kind="Internal").ap()

    ctx = ExitStack()
    k.ctx = ctx

    class Sem:
        def __init__(self, name):
            self.h = ctx.enter_context(nc.semaphore(name))
            self.n = 0

    k.sd = Sem("sd")
    k.sg = Sem("sg")
    k.sa = Sem("sa")
    k.sv2 = Sem("sv2")
    k.sv = Sem("sv")
    k.sc = Sem("sc")
    k.sp = Sem("sp")
    k.sq = Sem("sq")
    k.sg2 = Sem("sg2")
    for _nm in ("z1", "z2", "z3", "z4", "z5", "z6", "z7", "z8",
                "z9", "za", "zb", "zc", "zd", "ze"):
        setattr(k, "s" + _nm, Sem("s" + _nm))
    k.all_sems = [k.sd, k.sg, k.sa, k.sv2, k.sv, k.sc, k.sp, k.sq,
                  k.sz1, k.sz2, k.sz3, k.sz4, k.sz5, k.sz6, k.sz7, k.sz8,
                  k.sz9, k.sza, k.szb, k.szc, k.szd, k.sze]

    DMA_SEM = {"sync": k.sd, "gpsimd": k.sg, "scalar": k.sa, "vector": k.sv2}
    CMP_SEM = {"vector": k.sv, "scalar": k.sc, "tensor": k.sp, "gpsimd": k.sq}
    k.last_wait = {n: {} for n in ("sync", "tensor", "vector", "scalar",
                                   "gpsimd")}

    def wait(eng_name, *sems_vals):
        best = {}
        for s, v in sems_vals:
            if v <= 0:
                continue
            if best.get(id(s), (None, -1))[1] < v:
                best[id(s)] = (s, v)
        pairs = []
        lw = k.last_wait[eng_name]
        for s, v in best.values():
            if lw.get(id(s), -1) >= v:
                continue
            lw[id(s)] = v
            pairs.append((s.h, v))
        if pairs:
            def fn(eng, pairs=pairs):
                for h, v in pairs:
                    eng.wait_ge(h, v)
            P.steps.append((eng_name, fn))

    def dma(queue, out, in_, sem=None, **kw):
        s = sem if sem is not None else DMA_SEM[queue]
        s.n += 16

        def fn(eng, out=out, in_=in_, kw=kw):
            eng.dma_start(out=out, in_=in_, **kw).then_inc(s.h, 16)
        P.steps.append((queue, fn))
        return (s, s.n)

    def comp(eng_name, emit):
        s = CMP_SEM[eng_name]
        s.n += 1

        def fn(eng, emit=emit):
            emit(eng).then_inc(s.h, 1)
        P.steps.append((eng_name, fn))
        return (s, s.n)

    def raw(eng_name, emit):
        P.steps.append((eng_name, lambda eng, emit=emit: emit(eng)))

    def barrier():
        allv = [(s, s.n) for s in k.all_sems] + [(k.sg2, k.sg2.n)]
        for en in ("sync", "tensor", "vector", "scalar", "gpsimd"):
            wait(en, *allv)

    def tap(name, src, queue="sync"):
        if name in k.dbg:
            wait(queue, *[(sem, sem.n) for sem in k.all_sems + [k.sg2]])
            t = dma(queue, k.dbg[name], src)
            wait(queue, t)

    def vf():
        wait("vector", (k.sv, k.sv.n))

    k.wait, k.dma, k.comp, k.raw = wait, dma, comp, raw
    k.barrier, k.tap, k.vf = barrier, tap, vf
    return k


def prep(k):
    nc, e = k.nc, k.ctx.enter_context
    wait, dma, comp, raw, vf = k.wait, k.dma, k.comp, k.raw, k.vf

    k.wtoB = e(_sb(nc, "wtoB", [128, 16 * DM], BF16))
    k.fcdB = e(_sb(nc, "fcdB", [128, 16 * NH], BF16))
    k.wrB = e(_sb(nc, "wrB", [128, 8 * NEXP], BF16))
    k.convD = e(_sb(nc, "convD", [128, 126 * 128], BF16))
    k.convB = e(_sb(nc, "convB", [128, 18], F32))
    k.identB = e(_sb(nc, "identB", [128, 128], BF16))
    k.identF = e(_sb(nc, "identF", [128, 128], F32))
    k.maskfB = e(_sb(nc, "maskfB", [128, 128], BF16))
    k.maskbB = e(_sb(nc, "maskbB", [128, 128], BF16))
    k.triliB = e(_sb(nc, "triliB", [128, 128], BF16))
    k.sel2h4B = e(_sb(nc, "sel2h4B", [4, 128], BF16))
    k.iota8B = e(_sb(nc, "iota8B", [128, NEXP], F32))
    k.iotakB = e(_sb(nc, "iotakB", [128, 8], F32))
    k.tilestB = e(_sb(nc, "tilestB", [NEXP, NTIL], F32))
    k.onescbB = e(_sb(nc, "onescbB", [128, 1], BF16))
    k.onesrbB = e(_sb(nc, "onesrbB", [1, 128], BF16))
    k.ones2bB = e(_sb(nc, "ones2bB", [2, 128], BF16))
    k.onesrfB = e(_sb(nc, "onesrfB", [1, 512], F32))
    k.dtbB = e(_sb(nc, "dtbB", [64, 1], F32))
    k.acolB = e(_sb(nc, "acolB", [64, 1], F32))
    k.dparrB = e(_sb(nc, "dparrB", [1, NH], F32))
    k.brB = e(_sb(nc, "brB", [1, NEXP], BF16))
    k.prefmB = e(_sb(nc, "prefmB", [NEXP, 1], F32))
    k.validB = e(_sb(nc, "validB", [NEXP, NEXP], I32))
    k.vrowsB = e(_sb(nc, "vrowsB", [NEXP, 1], I32))
    k.zrowB = e(_sb(nc, "zrowB", [1, 128], BF16))
    k.ones512b = e(_sb(nc, "ones512b", [1, 512], BF16))
    k.zer88 = e(_sb(nc, "zer88", [NEXP, NEXP], F32))
    k.cvalB = e(_sb(nc, "cvalB", [128, 8], F32))
    for _i, _v in enumerate((1e-5, 1.0, -1.0, -0.5, 1.0 / 2048.0)):
        nc.const_aps.register(F32, _v, k.cvalB[:, _i:_i + 1])
    k.jobrB = e(_sb(nc, "jobrB", [NJOBS, 1], I32))
    k.parrB = e(_sb(nc, "parrB", [2, 1], I32))
    k.cntsB = e(_sb(nc, "cntsB", [NJOBS, NEXP], F32))
    k.lpcols = e(_sb(nc, "lpcols", [128, NCH], F32))
    k.gcols = e(_sb(nc, "gcols", [128, NCH], F32))
    k.icols = e(_sb(nc, "icols", [128, NCH], F32))
    k.erhs = [e(_sb(nc, f"erhs{i}", [99, 4096], BF16)) for i in (0, 1)]

    loads = []
    for dst, src in ((k.identB, k.identb), (k.identF, k.identf),
                     (k.maskfB, k.maskf),
                     (k.maskbB, k.maskb), (k.triliB, k.trili),
                     (k.sel2h4B, k.sel2h4), (k.iota8B, k.iota8r),
                     (k.iotakB, k.iotak), (k.tilestB, k.tilest),
                     (k.onescbB, k.onescb), (k.onesrbB, k.onesrb),
                     (k.ones2bB, k.ones2b), (k.onesrfB, k.onesrf),
                     (k.dtbB, k.dtb2), (k.dparrB, k.dparr),
                     (k.prefmB, k.prefm), (k.validB, k.validm),
                     (k.vrowsB, k.vrows),
                     (k.jobrB, k.jobrows),
                     (k.parrB, k.parrow), (k.convB, k.conv_bc),
                     (k.cvalB, k.cvals)):
        loads.append(dma("sync", dst[:, :], src))
    for i in (0, 1):
        loads.append(dma("sync", k.erhs[i][0:96, :], k.sel3))
    t_small = loads[-1]

    with _sb(nc, "pf32", [128, FIP], F32) as pf32, \
         _sb(nc, "pbf", [128, FIP], BF16) as pbf, \
         _sb(nc, "pf32g", [128, DM], F32) as pf32g, \
         _sb(nc, "pbfg", [128, DM], BF16) as pbfg, \
         _sb(nc, "alog_t", [64, 1], F32) as alog_t, \
         _sb(nc, "convW", [128, 126], F32) as convW, \
         _sb(nc, "brf", [1, NEXP], F32) as brf, \
         _sb(nc, "nwcB", [128, 16], F32) as nwcB, \
         _sb(nc, "bsp", [NEXP, DM], F32) as bsp, \
         _sb(nc, "bhi", [NEXP, DM], BF16) as bhi, \
         _sb(nc, "blo", [NEXP, DM], BF16) as blo:
        dma("sync", alog_t[:, :], k.alog2)
        dma("sync", brf[:, :], k.br)
        dma("sync", convW[:, :], k.conv_wc)
        tinit = dma("sync", nwcB[:, :], k.nwc)
        wait("scalar", tinit)
        c1 = comp("scalar", lambda s: s.activation(
            out=k.acolB[:, :], in_=alog_t[:, :], func=AF.Exp))
        wait("vector", c1)
        comp("vector", lambda v: v.tensor_scalar_mul(
            k.acolB[:, :], k.acolB[:, :], -1.0))
        wait("vector", tinit)
        comp("vector", lambda v: v.tensor_copy(out=k.brB[:, :],
                                               in_=brf[:, :]))
        for idx in range(126):
            raw("vector", lambda v, idx=idx: v.tensor_scalar(
                out=k.convD[:, idx * 128:(idx + 1) * 128],
                in0=k.identB[:, :], scalar1=convW[:, idx:idx + 1],
                scalar2=None, op0=ALU.mult))
        comp("vector", lambda v: v.memset(k.zer88[:, :], 0.0))
        comp("vector", lambda v: v.memset(k.cntsB[:, :], 0.0))
        comp("vector", lambda v: v.memset(k.zrowB[:, :], 0.0))
        comp("vector", lambda v: v.memset(k.ones512b[:, :], 1.0))
        comp("vector", lambda v: v.memset(k.erhs[0][96:99, :], 1.0))
        comp("vector", lambda v: v.memset(k.erhs[1][96:99, :], 1.0))

        for i in range(16):
            wait("sync", (k.sv, k.sv.n), (k.sc, k.sc.n))
            t1 = dma("sync", pf32[:, 0:DM], k.wt_out[i * 128:(i + 1) * 128, :],
                     sem=k.sz1)
            wait("vector", t1)
            comp("vector", lambda v, i=i: v.tensor_scalar(
                out=k.wtoB[:, i * DM:(i + 1) * DM], in0=pf32[:, 0:DM],
                scalar1=nwcB[:, i:i + 1], scalar2=None, op0=ALU.mult))
        for i in range(16):
            wait("sync", (k.sv, k.sv.n))
            t1 = dma("sync", pf32[:, 0:NH], k.fcdt[i * 128:(i + 1) * 128, :],
                     sem=k.sz1)
            wait("vector", t1)
            comp("vector", lambda v, i=i: v.tensor_copy(
                out=k.fcdB[:, i * NH:(i + 1) * NH], in_=pf32[:, 0:NH]))
        for i in range(8):
            wait("sync", (k.sv, k.sv.n))
            t1 = dma("sync", pf32[:, 0:NEXP], k.wr[i * 128:(i + 1) * 128, :],
                     sem=k.sz1)
            wait("vector", t1)
            comp("vector", lambda v, i=i: v.tensor_copy(
                out=k.wrB[:, i * NEXP:(i + 1) * NEXP], in_=pf32[:, 0:NEXP]))
        for i in range(8):
            wait("sync", (k.sv, k.sv.n), (k.sa, k.sa.n), (k.sc, k.sc.n))
            t1 = dma("sync", pf32[:, :], k.wt_in[i * 128:(i + 1) * 128, :],
                     sem=k.sz1)
            wait("scalar", t1)
            c2 = comp("scalar", lambda s: s.copy(out=pbf[:, :], in_=pf32[:, :]))
            wait("scalar", c2)
            td = dma("scalar", k.wtin_bf[i * 128:(i + 1) * 128, :], pbf[:, :])
            wait("scalar", td)
        for src, dst in ((k.w1f, k.w1bf), (k.w2f, k.w2bf)):
            for i in range(64):
                t1 = dma("gpsimd", pf32g[:, :], src[i * 128:(i + 1) * 128, :])
                wait("gpsimd", t1)
                c2 = comp("gpsimd", lambda g: g.tensor_copy(
                    out=pbfg[:, :], in_=pf32g[:, :]))
                wait("gpsimd", c2)
                td = dma("gpsimd", dst[i * 128:(i + 1) * 128, :], pbfg[:, :])
                wait("gpsimd", td)
        for src, dhi, dlo in ((k.b1r, k.b1hi, k.b1lo),
                              (k.b2r, k.b2hi, k.b2lo)):
            wait("sync", (k.sv, k.sv.n), (k.sa, k.sa.n))
            t1 = dma("sync", bsp[:, :], src, sem=k.sz1)
            wait("vector", t1)
            ch = comp("vector", lambda v: v.tensor_copy(out=bhi[:, :],
                                                        in_=bsp[:, :]))
            vf()
            cm = comp("vector", lambda v: v.tensor_tensor(
                out=bsp[:, :], in0=bsp[:, :], in1=bhi[:, :], op=ALU.subtract))
            vf()
            c2 = comp("vector", lambda v: v.tensor_copy(out=blo[:, :],
                                                        in_=bsp[:, :]))
            wait("scalar", c2)
            d1 = dma("scalar", dhi, bhi[:, :])
            d2 = dma("scalar", dlo, blo[:, :])
            wait("vector", d1, d2)
        k.barrier()


def attn_job(k, job):
    nc, e = k.nc, k.ctx.enter_context
    wait, dma, comp, raw, tap = k.wait, k.dma, k.comp, k.raw, k.tap
    vf = k.vf
    own = (job == 0)
    u = k.u_jobs[job * L:(job + 1) * L, :]

    with ExitStack() as S:
        es = S.enter_context
        bcpad = es(_sb(nc, "bcpad", [128, 2 * (L + 2)], BF16))
        dtr = es(_sb(nc, "dtr", [64, L], F32))
        tbuf = es(_sb(nc, "tbuf", [64, L], F32))
        rbuf = es(_sb(nc, "rbuf", [64, L], F32))
        rpack = [es(_sb(nc, f"rpack{d}", [99, L], BF16))
                 for d in (0, 1)]
        wbias = es(_sb(nc, "wbias", [64, 16], F32))
        biasfm = es(_sb(nc, "biasfm", [64, 16], F32))
        fcdTm = es(_sb(nc, "fcdTm", [128, 16 * NH], F32))

        SA = ExitStack()
        ea = SA.enter_context
        ulnT = ea(_sb(nc, "ulnT", [128, 8 * L], BF16))
        x0t = ea(_sb(nc, "x0t", [128, DM], F32))
        ulnt = ea(_sb(nc, "ulnt", [128, DM], BF16))
        stat = ea(_sb(nc, "stat", [128, 12], F32))
        mv = ea(_sb(nc, "mv", [128, 2], F32))
        rstd = ea(_sb(nc, "rstd", [128, 1], F32))
        tstage = ea(_sb(nc, "tstage", [128, 512], BF16))
        SAz = ExitStack()
        wtZ = SAz.enter_context(_sb(nc, "wtZ", [128, 8 * DI], BF16))
        zst = SAz.enter_context(_sb(nc, "zst", [128, DI], BF16))

        cpad = comp("vector", lambda v: v.memset(bcpad[:, :], 0.0))
        wait("tensor", cpad)
        wait("scalar", cpad)

        tz = []
        for kc in range(8):
            tz.append(dma("sync", wtZ[:, kc * DI:(kc + 1) * DI],
                          k.wtin_bf[kc * 128:(kc + 1) * 128, 0:DI],
                          sem=k.sz2))

        with _ps(nc, "ptpA", [128, 512], BF16) as ptpA:
            for c in range(NCH):
                wait("sync", (k.sv, k.sv.n))
                t1 = dma("sync", x0t[:, :], u[c * 128:(c + 1) * 128, :],
                         sem=k.sz3)
                wait("vector", t1, (k.sp, k.sp.n))
                raw("vector", lambda v: v.bn_stats(out=stat[:, 0:6],
                                                   in_=x0t[:, 0:512]))
                comp("vector", lambda v: v.bn_stats(out=stat[:, 6:12],
                                                    in_=x0t[:, 512:1024]))
                vf()
                c1 = comp("vector", lambda v: v.bn_aggr(out=mv[:, :],
                                                        in_=stat[:, :]))
                wait("scalar", c1)
                c1b = comp("scalar", lambda s: s.activation(
                    out=rstd[:, :], in_=mv[:, 1:2], func=AF.Ln, bias=1e-5))
                wait("scalar", c1b)
                c2 = comp("scalar", lambda s: s.activation(
                    out=rstd[:, :], in_=rstd[:, :], func=AF.Exp, scale=-0.5))
                wait("vector", c2)
                c3 = comp("vector", lambda v: v.tensor_scalar(
                    out=ulnt[:, :], in0=x0t[:, :], scalar1=mv[:, 0:1],
                    scalar2=rstd[:, :], op0=ALU.subtract, op1=ALU.mult))
                wait("tensor", c3)
                for g4 in range(2):
                    for k2 in range(4):
                        kc = g4 * 4 + k2
                        em = (lambda t, kc=kc, k2=k2: t.transpose(
                            ptpA[:, k2 * 128:(k2 + 1) * 128],
                            ulnt[:, kc * 128:(kc + 1) * 128], k.identB[:, :]))
                        if k2 == 3:
                            c5 = comp("tensor", em)
                        else:
                            raw("tensor", em)
                    wait("scalar", c5)
                    dstv = sview(ulnT[:, g4 * 4 * L + c * 128:],
                                 (L, 4), (1, 128))
                    c6 = comp("scalar", lambda s, dstv=dstv: s.copy(
                        out=dstv, in_=ptpA[:, 0:512].rearrange(
                            "p (a c) -> p a c", a=4)))
                    wait("tensor", c6)

        tap("ulnT", ulnT[:, :])
        if STOP_AFTER == "p1a":
            k.barrier()
            SAz.close()
            SA.close()
            return

        with _ps(nc, "psZ", [128, 512], F32) as psZ:
            wait("tensor", *tz)
            for c in range(NCH):
                for zb in range(4):
                    for kc in range(8):
                        em = (lambda t, c=c, zb=zb, kc=kc: t.matmul(
                            psZ[:, :],
                            ulnT[:, kc * L + c * 128:kc * L + (c + 1) * 128],
                            wtZ[:, kc * DI + zb * 512:kc * DI + (zb + 1) * 512],
                            start=(kc == 0), stop=(kc == 7)))
                        if kc == 7:
                            c1 = comp("tensor", em)
                        else:
                            raw("tensor", em)
                    wait("scalar", c1)
                    c2 = comp("scalar", lambda s, zb=zb: s.activation(
                        out=zst[:, zb * 512:(zb + 1) * 512], in_=psZ[:, :],
                        func=AF.Silu))
                    wait("tensor", c2)
                wait("scalar", c2)
                t3 = dma("scalar", k.zs_sp[c], zst[:, :])
                wait("scalar", t3)

        k.barrier()
        SAz.close()
        if STOP_AFTER == "p1b":
            SA.close()
            return
        xbct = ea(_sb(nc, "xbct", [128, L + 6], BF16))
        xcv = ea(_sb(nc, "xcv", [128, L], BF16))
        wtS = ea(_sb(nc, "wtS", [128, 8 * 128], BF16))
        raw("vector", lambda v: v.memset(xbct[:, 0:3], 0.0))
        cxp = comp("vector", lambda v: v.memset(xbct[:, L + 3:L + 6], 0.0))
        wait("tensor", cxp)
        wait("scalar", cxp)
        with _ps(nc, "psF", [128, 512], F32) as psF, \
             _ps(nc, "psC", [128, 512], F32) as psC, \
             _ps(nc, "ptpB", [128, 512], BF16) as ptpB:
            for j in range(16, 35):
                jc = j - 16
                fw = 128 if j < 34 else 64
                wait("sync", (k.sp, k.sp.n))
                tw = []
                for kc in range(8):
                    tw.append(dma("sync", wtS[:, kc * 128:kc * 128 + fw],
                                  k.wtin_bf[kc * 128:(kc + 1) * 128,
                                            j * 128:j * 128 + fw],
                                  sem=k.sz4))
                wait("tensor", *tw)
                for tb in range(4):
                    for kc in range(8):
                        em = (lambda t, tb=tb, kc=kc, fw=fw: t.matmul(
                            psF[0:fw, :], wtS[:, kc * 128:kc * 128 + fw],
                            ulnT[:, kc * L + tb * 512:kc * L + (tb + 1) * 512],
                            start=(kc == 0), stop=(kc == 7)))
                        if kc == 7:
                            c1 = comp("tensor", em)
                        else:
                            raw("tensor", em)
                    if j < 34:
                        wait("scalar", c1)
                        c2 = comp("scalar", lambda s, tb=tb: s.copy(
                            out=xbct[:, 3 + tb * 512:3 + (tb + 1) * 512],
                            in_=psF[:, :]))
                        wait("tensor", c2)
                    else:
                        wait("vector", c1)
                        c2 = comp("vector", lambda v, tb=tb: v.tensor_copy(
                            out=dtr[:, tb * 512:(tb + 1) * 512],
                            in_=psF[0:64, :]))
                        wait("tensor", c2)
                if j == 34:
                    break
                for tb in range(4):
                    for tp in range(7):
                        em = (lambda t, jc=jc, tb=tb, tp=tp: t.matmul(
                            psC[:, :],
                            k.convD[:, (jc * 7 + tp) * 128:(jc * 7 + tp + 1) * 128],
                            xbct[:, tp + tb * 512:tp + (tb + 1) * 512],
                            start=(tp == 0), stop=(tp == 6)))
                        if tp == 6:
                            c1 = comp("tensor", em)
                        else:
                            raw("tensor", em)
                    wait("scalar", c1)
                    c2 = comp("scalar", lambda s, jc=jc, tb=tb: s.activation(
                        out=xcv[:, tb * 512:(tb + 1) * 512], in_=psC[:, :],
                        func=AF.Silu, bias=k.convB[:, jc:jc + 1]))
                    wait("tensor", c2)
                if j == 32:
                    tap("xbct32", xbct[:, :])
                    tap("xcv32", xcv[:, :])
                if jc <= 15:
                    wait("scalar", c2)
                    t3 = dma("scalar", k.xc_sp[jc], xcv[:, :])
                    wait("scalar", t3)
                    for tt0 in range(0, 16, 4):
                        for k2 in range(4):
                            em = (lambda t, tt0=tt0, k2=k2: t.transpose(
                                ptpB[:, k2 * 128:(k2 + 1) * 128],
                                xcv[:, (tt0 + k2) * 128:(tt0 + k2 + 1) * 128],
                                k.identB[:, :]))
                            if k2 == 3:
                                c4 = comp("tensor", em)
                            else:
                                raw("tensor", em)
                        wait("scalar", c4)
                        c5 = comp("scalar", lambda s: s.copy(
                            out=tstage[:, :], in_=ptpB[:, 0:512]))
                        wait("tensor", c5)
                        wait("scalar", c5)
                        t6 = dma("scalar",
                                 k.xt_sp[tt0:tt0 + 4, jc].rearrange(
                                     "a p c -> p a c"),
                                 tstage[:, :].rearrange("p (a c) -> p a c",
                                                        a=4))
                        wait("scalar", t6)
                else:
                    r = jc - 16
                    wait("vector", c2)
                    c4 = comp("vector", lambda v, r=r: v.tensor_copy(
                        out=bcpad[:, r * (L + 2) + 1:r * (L + 2) + 1 + L],
                        in_=xcv[:, :]))
                    wait("scalar", c4)

        tap("dtr", dtr[:, :])
        tap("bcpad", bcpad[:, :])
        if STOP_AFTER == "p1c":
            k.barrier()
            SA.close()
            return

        with _ps(nc, "pfcd", [NH, L], F32) as pfcd, \
             _ps(nc, "ptpC", [128, NH], F32) as ptpC, \
             _sb(nc, "fcdF", [NH, L], F32) as fcdF:
            for jc in range(16):
                wait("sync", (k.sp, k.sp.n))
                t1 = dma("sync", xcv[:, :], k.xc_sp[jc], sem=k.sz5)
                wait("tensor", t1)
                for tb in range(4):
                    em = (lambda t, jc=jc, tb=tb: t.matmul(
                        pfcd[:, tb * 512:(tb + 1) * 512],
                        k.fcdB[:, jc * NH:(jc + 1) * NH],
                        xcv[:, tb * 512:(tb + 1) * 512],
                        start=(jc == 0), stop=False))
                    if tb == 3:
                        comp("tensor", em)
                    else:
                        raw("tensor", em)
            for nb in range(4):
                em = (lambda t, nb=nb: t.matmul(
                    pfcd[:, nb * 512:(nb + 1) * 512],
                    k.dparrB[:, :],
                    k.onesrfB[:, :],
                    start=False, stop=True))
                if nb == 3:
                    c1 = comp("tensor", em)
                else:
                    raw("tensor", em)
            wait("vector", c1)
            c2 = comp("vector", lambda v: v.tensor_copy(out=fcdF[:, :],
                                                        in_=pfcd[:, :]))
            wait("tensor", c2)
            for tt in range(16):
                c3 = comp("tensor", lambda t, tt=tt: t.transpose(
                    ptpC[:, :], fcdF[:, tt * 128:(tt + 1) * 128],
                    k.identF[0:32, 0:32]))
                wait("vector", c3)
                c4 = comp("vector", lambda v, tt=tt: v.tensor_copy(
                    out=fcdTm[:, tt * NH:(tt + 1) * NH], in_=ptpC[:, :]))
                wait("tensor", c4)
        tap("fcdTm", fcdTm[:, :])
        k.barrier()
        SA.close()
        if STOP_AFTER == "p1":
            return

        SB = ExitStack()
        eb = SB.enter_context
        dts = eb(_sb(nc, "dts", [64, L], F32))
        tmpa = eb(_sb(nc, "tmpa", [64, L], F32))
        lndt = eb(_sb(nc, "lndt", [64, L], F32))
        ab = eb(_sb(nc, "ab", [64, L], F32))
        incl = eb(_sb(nc, "incl", [64, L], F32))
        acsb = eb(_sb(nc, "acsb", [64, L], F32))
        zero64 = eb(_sb(nc, "zero64", [64, L], F32))
        shi = eb(_sb(nc, "shi", [64, L], BF16))
        smid = eb(_sb(nc, "smid", [64, L], BF16))
        slo = eb(_sb(nc, "slo", [64, L], BF16))
        bnd = eb(_sb(nc, "bnd", [64, 17], F32))
        lam = eb(_sb(nc, "lam", [64, 16], F32))
        lamsp = eb(_sb(nc, "lamsp", [64, 16], BF16))

        c0 = comp("vector", lambda v: v.memset(zero64[:, :], 0.0))
        cs1 = comp("scalar", lambda s: s.activation(
            out=dts[:, :], in_=dtr[:, :], func=AF.Identity,
            bias=k.dtbB[:, :]))
        cs2 = comp("scalar", lambda s: s.activation(
            out=tmpa[:, :], in_=dtr[:, :], func=AF.Abs, bias=k.dtbB[:, :]))
        wait("scalar", cs2)
        cs3 = comp("scalar", lambda s: s.activation(
            out=tmpa[:, :], in_=tmpa[:, :], func=AF.Exp, scale=-1.0))
        wait("scalar", cs3)
        cs4 = comp("scalar", lambda s: s.activation(
            out=tmpa[:, :], in_=tmpa[:, :], func=AF.Ln, bias=1.0))
        wait("vector", cs1, cs4)
        cv1 = comp("vector", lambda v: v.tensor_scalar_max(
            dts[:, :], dts[:, :], 0.0))
        wait("vector", cv1)
        cv2 = comp("vector", lambda v: v.tensor_add(
            dts[:, :], dts[:, :], tmpa[:, :]))
        wait("scalar", cv2)
        comp("scalar", lambda s: s.activation(out=lndt[:, :], in_=dts[:, :],
                                              func=AF.Ln))
        wait("vector", cv2)
        cv3 = comp("vector", lambda v: v.tensor_scalar(
            out=ab[:, :], in0=dts[:, :], scalar1=k.acolB[:, :], scalar2=None,
            op0=ALU.mult))
        wait("vector", cv3, c0)
        cv4 = comp("vector", lambda v: v.tensor_tensor_scan(
            out=incl[:, :], data0=ab[:, :], data1=zero64[:, :], initial=0.0,
            op0=ALU.add, op1=ALU.add))
        wait("vector", cv4)
        cv5 = comp("vector", lambda v: v.tensor_copy(out=acsb[:, :],
                                                     in_=incl[:, :]))
        wait("vector", cv5)
        cv6 = comp("vector", lambda v: v.tensor_tensor(
            out=acsb[32:64, :], in0=incl[32:64, :], in1=ab[32:64, :],
            op=ALU.subtract))
        raw("vector", lambda v: v.memset(tbuf[0:32, 0:1], 0.0))
        wait("vector", cv6)
        raw("vector", lambda v: v.tensor_copy(out=tbuf[0:32, 1:L],
                                              in_=acsb[0:32, 0:L - 1]))
        raw("vector", lambda v: v.tensor_scalar_mul(
            tbuf[32:64, 0:L - 1], acsb[32:64, 1:L], -1.0))
        cv7 = comp("vector", lambda v: v.tensor_scalar_mul(
            tbuf[32:64, L - 1:L], incl[32:64, L - 1:L], -1.0))
        wc1 = (k.sc, k.sc.n)
        wait("vector", wc1)
        raw("vector", lambda v: v.tensor_tensor(
            out=rbuf[0:32, :], in0=lndt[0:32, :], in1=acsb[0:32, :],
            op=ALU.subtract))
        cv8 = comp("vector", lambda v: v.tensor_tensor(
            out=rbuf[32:64, :], in0=lndt[32:64, :], in1=acsb[32:64, :],
            op=ALU.add))
        wait("vector", cv8)

        def split3(src, reuse):
            comp("vector", lambda v: v.tensor_copy(out=shi[:, :],
                                                   in_=src[:, :]))
            vf()
            comp("vector", lambda v: v.tensor_tensor(
                out=tmpa[:, :], in0=src[:, :], in1=shi[:, :],
                op=ALU.subtract))
            vf()
            comp("vector", lambda v: v.tensor_copy(out=smid[:, :],
                                                   in_=tmpa[:, :]))
            vf()
            comp("vector", lambda v: v.tensor_tensor(
                out=tmpa[:, :], in0=tmpa[:, :], in1=smid[:, :],
                op=ALU.subtract))
            vf()
            return comp("vector", lambda v: v.tensor_copy(out=slo[:, :],
                                                          in_=tmpa[:, :]))

        cr = split3(rbuf, None)
        wait("vector", cr)
        raw("vector", lambda v: v.memset(rpack[0][96:99, :], 1.0))
        crm = comp("vector", lambda v: v.memset(rpack[1][96:99, :], 1.0))
        wait("sync", cr, crm)
        rp = []
        for d in (0, 1):
            rp.append(dma("sync", rpack[d][0:32, :],
                          shi[d * 32:(d + 1) * 32, :], sem=k.szd))
            rp.append(dma("sync", rpack[d][32:64, :],
                          smid[d * 32:(d + 1) * 32, :], sem=k.szd))
            rp.append(dma("sync", rpack[d][64:96, :],
                          slo[d * 32:(d + 1) * 32, :], sem=k.szd))
        wait("vector", *rp)
        ct = split3(tbuf, None)
        wait("sync", ct)
        tp1 = dma("sync", k.tfl[0:64, :], shi[:, :], sem=k.szd)
        tp2 = dma("sync", k.tfl[64:128, :], smid[:, :], sem=k.szd)
        tp3 = dma("sync", k.tfl[128:192, :], slo[:, :], sem=k.szd)
        wait("vector", tp1, tp2, tp3)

        raw("vector", lambda v: v.memset(bnd[:, 0:1], 0.0))
        raw("vector", lambda v: v.tensor_copy(
            out=bnd[0:32, 1:17], in_=sview(acsb[0:32, 127:], (128, 16))))
        raw("vector", lambda v: v.tensor_copy(
            out=bnd[32:64, 1:16], in_=sview(acsb[32:64, 128:], (128, 15))))
        cb1 = comp("vector", lambda v: v.tensor_copy(
            out=bnd[32:64, 16:17], in_=incl[32:64, L - 1:L]))
        wait("vector", cb1)
        raw("vector", lambda v: v.tensor_copy(out=wbias[0:32, :],
                                              in_=bnd[0:32, 1:17]))
        raw("vector", lambda v: v.tensor_scalar_mul(
            wbias[32:64, :], bnd[32:64, 0:16], -1.0))
        raw("vector", lambda v: v.tensor_scalar_mul(
            biasfm[0:32, :], bnd[0:32, 0:16], -1.0))
        raw("vector", lambda v: v.tensor_copy(out=biasfm[32:64, :],
                                              in_=bnd[32:64, 1:17]))
        cb2 = comp("vector", lambda v: v.tensor_tensor(
            out=lam[:, :], in0=bnd[:, 1:17], in1=bnd[:, 0:16],
            op=ALU.subtract))
        wait("scalar", cb2)
        cb3 = comp("scalar", lambda s: s.activation(out=lam[:, :],
                                                    in_=lam[:, :],
                                                    func=AF.Exp))
        wait("vector", cb3)
        comp("vector", lambda v: v.tensor_copy(out=lamsp[:, :], in_=lam[:, :]))
        vf()
        cb4 = comp("vector", lambda v: v.tensor_tensor(
            out=lam[:, :], in0=lam[:, :], in1=lamsp[:, :], op=ALU.subtract))
        wait("sync", cb4)
        lp1 = dma("sync", k.lfl[0:64, :], lamsp[:, :], sem=k.szd)
        wait("vector", lp1)
        cb5 = comp("vector", lambda v: v.tensor_copy(out=lamsp[:, :],
                                                     in_=lam[:, :]))
        wait("sync", cb5)
        lp2 = dma("sync", k.lfl[64:128, :], lamsp[:, :], sem=k.szd)
        wait("sync", lp2)
        tap("dts", dts[:, :])
        tap("ab", ab[:, :])
        tap("acolB", k.acolB[:, :])
        tap("dtbB", k.dtbB[:, :])
        tap("acsb", acsb[:, :])
        tap("tbuf", tbuf[:, :])
        tap("rbuf", rbuf[:, :])
        tap("bnd", bnd[:, :])
        k.barrier()
        SB.close()
        if STOP_AFTER == "p2":
            return

        SC = ExitStack()
        ec = SC.enter_context
        xtm = ec(_sb(nc, "xtm", [128, DI], BF16))
        wfm = ec(_sb(nc, "wfm", [64, 128], F32))
        wtm = ec(_sb(nc, "wtm", [128, 64], F32))
        btm = ec(_sb(nc, "btm", [128, 64], BF16))
        xpp = ec(_sb(nc, "xpp", [128, DI], BF16))
        slocp = ec(_sb(nc, "slocp", [128, 1024], F32))
        sacc = ec(_sb(nc, "sacc", [128, 1024], F32))
        stmp = ec(_sb(nc, "stmp", [128, 1024], F32))
        sbb = ec(_sb(nc, "sbb", [128, 1024], BF16))
        lam4 = ec(_sb(nc, "lam4", [4, 16], BF16))
        lrep = ec(_sb(nc, "lrep", [4, 1024], BF16))

        czs = comp("vector", lambda v: v.memset(sacc[:, :], 0.0))

        def lam_mm(d, c, psL):
            wait("sync", (k.sv, k.sv.n))
            for sp in (0, 1):
                src = bass.AP(k.lfl.tensor, sp * 1024 + d * 32 * 16 + c,
                              [[16, 2], [2 * 16, 16]])
                t1 = dma("sync", lam4[2 * sp:2 * sp + 2, :], src, sem=k.sz6,
                         allow_slow_non_contiguous=True)
            wait("vector", t1, (k.sp, k.sp.n))
            c1 = comp("vector", lambda v: v.tensor_copy(
                out=lrep[:, :], in_=sview(lam4[:, 0:], (1, 16), (0, 64))))
            wait("tensor", c1)
            raw("tensor", lambda t: t.matmul(
                psL[:, 0:512], k.sel2h4B[:, :], lrep[:, 0:512],
                start=True, stop=True))
            return comp("tensor", lambda t: t.matmul(
                psL[:, 512:1024], k.sel2h4B[:, :], lrep[:, 512:1024],
                start=True, stop=True))

        with _ps(nc, "psS", [64, DI], F32) as psS, \
             _ps(nc, "psL", [128, 1024], F32) as psL, \
             _ps(nc, "ptpD", [128, 64], F32) as ptpD, \
             _ps(nc, "ptpE", [128, 64], BF16) as ptpE:
            for c in range(NCH):
                wait("sync", (k.sv, k.sv.n), (k.sp, k.sp.n))
                tx = []
                for jc in range(16):
                    tx.append(dma("sync", xtm[:, jc * 128:(jc + 1) * 128],
                                  k.xt_sp[c, jc], sem=k.sz7))
                wait("scalar", (k.sp, k.sp.n))
                cw = comp("scalar", lambda s, c=c: s.activation(
                    out=wfm[:, :], in_=rbuf[:, c * 128:(c + 1) * 128],
                    func=AF.Exp, bias=wbias[:, c:c + 1]))
                wait("tensor", cw)
                ct1 = comp("tensor", lambda t: t.transpose(
                    ptpD[:, :], wfm[:, :], k.identF[0:64, 0:64]))
                wait("vector", ct1)
                cv1 = comp("vector", lambda v: v.tensor_copy(out=wtm[:, :],
                                                             in_=ptpD[:, :]))
                wait("vector", *tx, cv1)
                for d in (0, 1):
                    cvx = comp("vector", lambda v, d=d: v.tensor_tensor(
                        out=xpp[:, :], in0=xtm[:, :],
                        in1=sview(wtm[:, d * 32:], (1, 32), (0, 64)),
                        op=ALU.mult))
                    ct2 = comp("tensor", lambda t, d=d, c=c: t.transpose(
                        ptpE[:, :],
                        bcpad[0:64, d * (L + 2) + 1 + c * 128:
                              d * (L + 2) + 1 + (c + 1) * 128],
                        k.identB[0:64, 0:64]))
                    wait("vector", ct2)
                    cv2 = comp("vector", lambda v: v.tensor_copy(
                        out=btm[:, :], in_=ptpE[:, :]))
                    wait("tensor", cv2, cvx, cv1)
                    for qv in range(4):
                        em = (lambda t, qv=qv: t.matmul(
                            psS[:, qv * 512:(qv + 1) * 512],
                            btm[:, :], xpp[:, qv * 512:(qv + 1) * 512],
                            start=True, stop=True))
                        if qv == 3:
                            ct3 = comp("tensor", em)
                        else:
                            raw("tensor", em)
                    wait("vector", ct3)
                    raw("vector", lambda v: v.tensor_copy(
                        out=slocp[0:64, :],
                        in_=sview(psS[0:64, 0:], (128, 16), (1, 64))))
                    cv3 = comp("vector", lambda v: v.tensor_copy(
                        out=slocp[64:128, :],
                        in_=sview(psS[0:64, 64:], (128, 16), (1, 64))))
                    if d == 0:
                        wait("vector", czs)
                        cv4 = comp("vector", lambda v: v.tensor_copy(
                            out=sbb[:, :], in_=sacc[:, :]))
                        wait("sync", cv4)
                        dma("sync", _spackap(k.sfin_sp, c, 0), sbb[0:64, :],
                            sem=k.sz8)
                        ts = dma("sync", _spackap(k.sfin_sp, c, 1),
                                 sbb[64:128, :], sem=k.sz8)
                        cl = lam_mm(0, c, psL)
                        wait("vector", cl, cv3, ts)
                        comp("vector", lambda v: v.tensor_tensor(
                            out=stmp[:, :], in0=sacc[:, :], in1=psL[:, :],
                            op=ALU.mult))
                        vf()
                        czs = comp("vector", lambda v: v.tensor_add(
                            sacc[:, :], stmp[:, :], slocp[:, :]))
                        wait("tensor", czs)
                    else:
                        wait("vector", cv3)
                        cv5 = comp("vector", lambda v: v.tensor_copy(
                            out=sbb[:, :], in_=slocp[:, :]))
                        wait("sync", cv5)
                        dma("sync", _spackap(k.sb_sp, c, 0), sbb[0:64, :],
                            sem=k.sz8)
                        tb2 = dma("sync", _spackap(k.sb_sp, c, 1),
                                  sbb[64:128, :], sem=k.sz8)
                        wait("vector", tb2)
                        wait("tensor", cv5)

            cz2 = comp("vector", lambda v: v.memset(stmp[:, :], 0.0))
            for c in range(NCH - 1, -1, -1):
                wait("sync", (k.sz8, k.sz8.n))
                dma("sync", sbb[0:64, :], _spackap(k.sb_sp, c, 0),
                    sem=k.sz8)
                t1 = dma("sync", sbb[64:128, :], _spackap(k.sb_sp, c, 1),
                         sem=k.sz8)
                wait("vector", t1, cz2)
                cv1 = comp("vector", lambda v: v.tensor_copy(
                    out=slocp[:, :], in_=sbb[:, :]))
                wait("vector", cv1)
                cv2 = comp("vector", lambda v: v.tensor_copy(
                    out=sbb[:, :], in_=stmp[:, :]))
                wait("sync", cv2)
                dma("sync", _spackap(k.sb_sp, c, 0), sbb[0:64, :],
                    sem=k.sz8)
                t2 = dma("sync", _spackap(k.sb_sp, c, 1), sbb[64:128, :],
                         sem=k.sz8)
                cl = lam_mm(1, c, psL)
                wait("vector", cl, t2)
                comp("vector", lambda v: v.tensor_tensor(
                    out=sacc[:, :], in0=stmp[:, :], in1=psL[:, :],
                    op=ALU.mult))
                vf()
                cz2 = comp("vector", lambda v: v.tensor_add(
                    stmp[:, :], sacc[:, :], slocp[:, :]))
                wait("tensor", cz2)
        k.barrier()
        SC.close()
        if STOP_AFTER == "p3":
            return

        SD = ExitStack()
        ed = SD.enter_context
        xtm2 = ed(_sb(nc, "xtm2", [128, DI], BF16))
        zst2 = ed(_sb(nc, "zst2", [128, DI], BF16))
        sfF = ed(_sb(nc, "sfF", [64, 2048], BF16))
        sfB = ed(_sb(nc, "sfB", [64, 2048], BF16))
        x0t2 = ed(_sb(nc, "x0t2", [128, DM], F32))
        msb = ed(_sb(nc, "msb", [128, 4096], BF16))
        cpp = ed(_sb(nc, "cpp", [64, 4096], BF16))
        etmp = ed(_sb(nc, "etmp", [128, 1024], BF16))
        gmsk = ed(_sb(nc, "gmsk", [128, 128], BF16))
        cpad0 = ed(_sb(nc, "cpad0", [64, 130], BF16))
        eflr = ed(_sb(nc, "eflr", [1, 1024], BF16))
        efm = ed(_sb(nc, "efm", [64, 128], F32))
        efmb = ed(_sb(nc, "efmb", [64, 128], BF16))
        yb = ed(_sb(nc, "yb", [128, DI], F32))
        ygated = ed(_sb(nc, "ygated", [128, DI], BF16))
        ygT = ed(_sb(nc, "ygT", [128, DI], BF16))
        x1b = ed(_sb(nc, "x1b", [128, DM], F32))
        uln2 = ed(_sb(nc, "uln2", [128, DM], BF16))
        u2T = ed(_sb(nc, "u2T", [128, DM], BF16))
        lgs = ed(_sb(nc, "lgs", [128, NEXP], F32))
        stat2 = ed(_sb(nc, "stat2", [128, 12], F32))
        mv2 = ed(_sb(nc, "mv2", [128, 2], F32))
        rcol = ed(_sb(nc, "rcol", [128, 1], F32))
        ngm = ed(_sb(nc, "ngm", [128, 1], F32))
        sexp = ed(_sb(nc, "sexp", [128, 1], F32))
        gcol = ed(_sb(nc, "gcol", [128, 1], F32))
        edump = ed(_sb(nc, "edump", [128, NEXP], F32))
        mx8 = ed(_sb(nc, "mx8", [128, NEXP], F32))
        mi8 = ed(_sb(nc, "mi8", [128, NEXP], U32))
        idxf = ed(_sb(nc, "idxf", [128, 1], F32))
        mskf = ed(_sb(nc, "mskf", [128, NEXP], F32))
        mskb16 = ed(_sb(nc, "mskb16", [128, NEXP], BF16))
        posb = ed(_sb(nc, "posb", [128, NEXP], F32))
        prefrow = ed(_sb(nc, "prefrow", [1, NEXP], F32))
        ctmp = ed(_sb(nc, "ctmp", [1, NEXP], F32))
        lpcols, gcols, icols = k.lpcols, k.gcols, k.icols
        poolr = ed(_sb(nc, "poolr", [128, DM], BF16))

        cpre = comp("vector", lambda v: v.memset(prefrow[:, :], 0.0))
        wait("tensor", cpre)

        with _ps(nc, "psY", [128, DI], F32) as psY, \
             _ps(nc, "psCnt", [1, NEXP], F32) as psCnt:
            for c in range(NCH):
                wait("sync", (k.sv, k.sv.n), (k.sp, k.sp.n))
                tx = [dma("sync", xtm2[:, jc * 128:(jc + 1) * 128],
                          k.xt_sp[c, jc], sem=k.sz9) for jc in range(16)]
                dma("sync", zst2[:, :], k.zs_sp[c], sem=k.sz9)
                dma("sync", sfF[:, :], k.sfin_sp[c], sem=k.sz9)
                dma("sync", sfB[:, :], k.sb_sp[c], sem=k.sz9)
                tx0 = dma("sync", x0t2[:, :], u[c * 128:(c + 1) * 128, :],
                          sem=k.sz9)
                tz2 = tsf = tsb = tx0

                wait("scalar", (k.sza, k.sza.n), (k.szb, k.szb.n))
                ce1 = comp("scalar", lambda s, c=c: s.activation(
                    out=efm[:, :], in_=tbuf[:, c * 128:(c + 1) * 128],
                    func=AF.Exp, bias=biasfm[:, c:c + 1]))
                wait("vector", ce1)
                ce2 = comp("vector", lambda v: v.tensor_copy(
                    out=efmb[:, :], in_=efm[:, :]))
                wait("sync", ce2)
                te = dma("sync", k.efl, efmb[:, :], sem=k.sza)

                for d in (0, 1):
                    coff = 0 if d == 0 else 2
                    vf()
                    cpw = comp("vector", lambda v, d=d, c=c: v.tensor_copy(
                        out=cpad0[:, :],
                        in_=bcpad[64:128, d * (L + 2) + c * 128:
                                  d * (L + 2) + c * 128 + 130]))
                    wait("sync", (k.sp, k.sp.n))
                    src = bass.AP(k.tfl.tensor, d * 32 * L + c * 128,
                                  [[64 * L, 3], [L, 32], [1, 128]])
                    tt = dma("sync", k.erhs[d][96:99, :], src, sem=k.szc)
                    with _ps(nc, "psG", [128, 128], F32) as psG:
                        wait("tensor", cpw)
                        cg1 = comp("tensor", lambda t, d=d, c=c,
                                   coff=coff: t.matmul(
                            psG[:, :],
                            bcpad[0:64, d * (L + 2) + 1 + c * 128:
                                  d * (L + 2) + 1 + (c + 1) * 128],
                            cpad0[:, coff:coff + 128],
                            start=True, stop=True))
                        wait("vector", cg1)
                        mk = k.maskfB if d == 0 else k.maskbB
                        cg2 = comp("vector", lambda v, mk=mk: v.tensor_tensor(
                            out=gmsk[:, :], in0=psG[:, :], in1=mk[:, :],
                            op=ALU.mult))
                        wait("tensor", cg2)
                    wait("vector", (k.sp, k.sp.n))
                    with _ps(nc, "psE", [128, 1024], F32) as psE:
                        for hg in range(4):
                            wait("tensor", tt, (k.sc, k.sc.n))
                            raw("tensor",
                                lambda t, d=d, c=c, hg=hg: t.matmul(
                                    psE[:, 0:512],
                                    rpack[d][:, c * 128:(c + 1) * 128],
                                    k.erhs[d][0:99,
                                              hg * 1024:hg * 1024 + 512],
                                    start=True, stop=True))
                            cm1 = comp("tensor",
                                       lambda t, d=d, c=c, hg=hg: t.matmul(
                                           psE[:, 512:1024],
                                           rpack[d][:, c * 128:(c + 1) * 128],
                                           k.erhs[d][0:99, hg * 1024 + 512:
                                                     (hg + 1) * 1024],
                                           start=True, stop=True))
                            wait("scalar", cm1)
                            cm2 = comp("scalar", lambda s: s.activation(
                                out=etmp[:, :], in_=psE[:, :], func=AF.Exp))
                            wait("vector", cm2, cg2)
                            cm3 = comp("vector",
                                       lambda v, hg=hg: v.tensor_tensor(
                                           out=msb[:, hg * 1024:(hg + 1) * 1024],
                                           in0=etmp[:, :],
                                           in1=sview(gmsk[:, 0:], (0, 8),
                                                     (1, 128)),
                                           op=ALU.mult))
                            wait("scalar", cm3)
                    wait("tensor", (k.sv, k.sv.n), tx0)
                    if d == 0:
                        for b4 in range(4):
                            raw("tensor", lambda t, b4=b4: t.matmul(
                                psY[:, b4 * 512:(b4 + 1) * 512],
                                k.zrowB[0:1, :], k.ones512b[0:1, :],
                                start=True, stop=False))
                    for h in range(32):
                        em = (lambda t, d=d, h=h: t.matmul(
                            psY[:, h * 64:(h + 1) * 64],
                            msb[:, (h // 8) * 1024 + (h % 8) * 128:
                                (h // 8) * 1024 + (h % 8) * 128 + 128],
                            xtm2[:, h * 64:(h + 1) * 64],
                            start=False, stop=False))
                        if h == 31:
                            cyd = comp("tensor", em)
                        else:
                            raw("tensor", em)
                    for qu in range(4):
                        wait("sync", (k.sv, k.sv.n), (k.sp, k.sp.n))
                        wait("vector", (k.sp, k.sp.n))
                        wait("sync", te)
                        tfr = dma("sync", eflr[:, :],
                                  bass.AP(k.efl.tensor,
                                          d * 32 * 128 + qu * 1024,
                                          [[128, 8], [1, 128]]),
                                  sem=k.szb)
                        wait("tensor", te, tfr)
                        with _ps(nc, "psX", [64, 1024], F32) as psX:
                            raw("tensor", lambda t: t.matmul(
                                psX[:, 0:512], k.onesrbB[0:1, 0:64],
                                eflr[0:1, 0:512],
                                start=True, stop=True))
                            cx1 = comp("tensor", lambda t: t.matmul(
                                psX[:, 512:1024], k.onesrbB[0:1, 0:64],
                                eflr[0:1, 512:1024],
                                start=True, stop=True))
                            wait("vector", cx1)
                            cx2 = comp("vector",
                                       lambda v, qu=qu, d=d, c=c,
                                       coff=coff: v.tensor_tensor(
                                           out=cpp[:, qu * 1024:
                                                   (qu + 1) * 1024],
                                           in0=sview(
                                               cpad0[:, coff:],
                                               (0, 8), (1, 128)),
                                           in1=psX[:, :], op=ALU.mult))
                            wait("tensor", cx2)
                    sf = sfF if d == 0 else sfB
                    wait("tensor", tsf if d == 0 else tsb, (k.sv, k.sv.n))
                    for h in range(32):
                        em = (lambda t, h=h, sf=sf, d=d: t.matmul(
                            psY[:, h * 64:(h + 1) * 64],
                            cpp[0:64, h * 128:(h + 1) * 128],
                            sf[0:64, h * 64:(h + 1) * 64],
                            start=False, stop=(d == 1 and h % 8 == 7)))
                        if h == 31:
                            cyo = comp("tensor", em)
                        else:
                            raw("tensor", em)

                if c == NCH - 1:
                    tap("msb", msb[:, :])
                    tap("gmsk", gmsk[:, :])
                    tap("cpp", cpp[:, :])
                    tap("sfF", sfF[:, :])
                    tap("sfB", sfB[:, :])
                wait("vector", cyo, (k.sc, k.sc.n))
                comp("vector", lambda v, c=c: v.tensor_tensor(
                    out=yb[:, :], in0=xtm2[:, :],
                    in1=sview(fcdTm[:, c * NH:], (1, 32), (0, 64)),
                    op=ALU.mult))
                vf()
                cy1 = comp("vector", lambda v: v.tensor_add(
                    yb[:, :], yb[:, :], psY[:, :]))
                wait("tensor", cy1)
                wait("scalar", cy1)
                cy2 = comp("scalar", lambda s: s.activation(
                    out=msb[:, 0:DI], in_=yb[:, :], func=AF.Square,
                    accum_out=rcol[:, :]))
                wait("scalar", cy2)
                cy2b = comp("scalar", lambda s: s.activation(
                    out=rcol[:, :], in_=rcol[:, :], func=AF.Ln,
                    scale=1.0 / DI, bias=1e-5))
                wait("scalar", cy2b)
                cy3 = comp("scalar", lambda s: s.activation(
                    out=rcol[:, :], in_=rcol[:, :], func=AF.Exp, scale=-0.5))
                if c == NCH - 1:
                    tap("yb", yb[:, :])
                wait("vector", cy3, tz2)
                cy4 = comp("vector", lambda v: v.scalar_tensor_tensor(
                    out=ygated[:, :], in0=yb[:, :], scalar=rcol[:, :],
                    in1=zst2[:, :], op0=ALU.mult, op1=ALU.mult))
                wait("tensor", cy4)
                if c == NCH - 1:
                    tap("ygated", ygated[:, :])
                with _ps(nc, "ptpF", [128, 512], BF16) as ptpF:
                    for tt0 in range(0, 16, 4):
                        for k2 in range(4):
                            em = (lambda t, tt0=tt0, k2=k2: t.transpose(
                                ptpF[:, k2 * 128:(k2 + 1) * 128],
                                ygated[:, (tt0 + k2) * 128:(tt0 + k2 + 1) * 128],
                                k.identB[:, :]))
                            if k2 == 3:
                                cy5 = comp("tensor", em)
                            else:
                                raw("tensor", em)
                        wait("scalar", cy5)
                        cy6 = comp("scalar", lambda s, tt0=tt0: s.copy(
                            out=ygT[:, tt0 * 128:(tt0 + 4) * 128],
                            in_=ptpF[:, 0:512]))
                        wait("tensor", cy6)
                with _ps(nc, "psO", [128, DM], F32) as psO:
                    wait("tensor", (k.sc, k.sc.n))
                    for kk in range(16):
                        for nb in (0, 1):
                            em = (lambda t, kk=kk, nb=nb: t.matmul(
                                psO[:, nb * 512:(nb + 1) * 512],
                                ygT[:, kk * 128:(kk + 1) * 128],
                                k.wtoB[:, kk * DM + nb * 512:
                                       kk * DM + (nb + 1) * 512],
                                start=(kk == 0), stop=(kk == 15)))
                            if kk == 15 and nb == 1:
                                cy7 = comp("tensor", em)
                            else:
                                raw("tensor", em)
                    wait("vector", cy7, tx0)
                    cy8 = comp("vector", lambda v: v.tensor_add(
                        x1b[:, :], psO[:, :], x0t2[:, :]))
                    wait("tensor", cy8)
                if own:
                    wait("sync", cy8)
                    t1s = dma("sync", k.x1_sp[c * 128:(c + 1) * 128, :],
                              x1b[:, :], sem=k.sze)
                wait("vector", cy8)
                raw("vector", lambda v: v.bn_stats(out=stat2[:, 0:6],
                                                   in_=x1b[:, 0:512]))
                comp("vector", lambda v: v.bn_stats(out=stat2[:, 6:12],
                                                    in_=x1b[:, 512:1024]))
                vf()
                cl1 = comp("vector", lambda v: v.bn_aggr(out=mv2[:, :],
                                                         in_=stat2[:, :]))
                wait("scalar", cl1)
                cl1b = comp("scalar", lambda s: s.activation(
                    out=rcol[:, :], in_=mv2[:, 1:2], func=AF.Ln, bias=1e-5))
                wait("scalar", cl1b)
                cl2 = comp("scalar", lambda s: s.activation(
                    out=rcol[:, :], in_=rcol[:, :], func=AF.Exp, scale=-0.5))
                wait("vector", cl2)
                cl3 = comp("vector", lambda v: v.tensor_scalar(
                    out=uln2[:, :], in0=x1b[:, :], scalar1=mv2[:, 0:1],
                    scalar2=rcol[:, :], op0=ALU.subtract, op1=ALU.mult))
                wait("tensor", cl3)
                with _ps(nc, "ptpG", [128, 512], BF16) as ptpG, \
                     _ps(nc, "psLG", [128, NEXP], F32) as psLG:
                    for g4 in range(2):
                        for k2 in range(4):
                            em = (lambda t, g4=g4, k2=k2: t.transpose(
                                ptpG[:, k2 * 128:(k2 + 1) * 128],
                                uln2[:, (g4 * 4 + k2) * 128:
                                     (g4 * 4 + k2 + 1) * 128],
                                k.identB[:, :]))
                            if k2 == 3:
                                cl4 = comp("tensor", em)
                            else:
                                raw("tensor", em)
                        wait("scalar", cl4)
                        cl5 = comp("scalar", lambda s, g4=g4: s.copy(
                            out=u2T[:, g4 * 512:(g4 + 1) * 512],
                            in_=ptpG[:, 0:512]))
                        wait("tensor", cl5)
                    for kk in range(8):
                        raw("tensor", lambda t, kk=kk: t.matmul(
                            psLG[:, :], u2T[:, kk * 128:(kk + 1) * 128],
                            k.wrB[:, kk * NEXP:(kk + 1) * NEXP],
                            start=(kk == 0), stop=False))
                    cl6 = comp("tensor", lambda t: t.matmul(
                        psLG[:, :], k.onesrbB[:, :], k.brB[:, :],
                        start=False, stop=True))
                    wait("vector", cl6)
                    cl7 = comp("vector", lambda v: v.tensor_copy(
                        out=lgs[:, :], in_=psLG[:, :]))
                    wait("tensor", cl7)
                wait("vector", cl7)
                cr1 = comp("vector", lambda v: v.reduce_max(
                    out=ngm[:, :], in_=lgs[:, :], axis=AX.X, negate=True))
                wait("scalar", cr1)
                cr2 = comp("scalar", lambda s: s.activation(
                    out=edump[:, :], in_=lgs[:, :], func=AF.Exp,
                    bias=ngm[:, :], accum_out=sexp[:, :]))
                wait("vector", cr2)
                raw("vector", lambda v: v.reciprocal(gcol[:, :], sexp[:, :]))
                comp("vector", lambda v: v.max(out=mx8[:, :], in_=lgs[:, :]))
                vf()
                cr3 = comp("vector", lambda v: v.max_index(
                    out=mi8[:, :], in_max=mx8[:, :], in_values=lgs[:, :]))
                wait("vector", cr3)
                cr4 = comp("vector", lambda v: v.tensor_copy(
                    out=idxf[:, :], in_=mi8[:, 0:1]))
                wait("vector", cr4)
                cr5 = comp("vector", lambda v: v.tensor_scalar(
                    out=mskf[:, :], in0=k.iota8B[:, :], scalar1=idxf[:, :],
                    scalar2=None, op0=ALU.is_equal))
                wait("vector", cr5)
                cr6 = comp("vector", lambda v: v.tensor_copy(
                    out=mskb16[:, :], in_=mskf[:, :]))
                wait("tensor", cr6)
                if own:
                    with _ps(nc, "psP", [128, NEXP], F32) as psP:
                        raw("tensor", lambda t: t.matmul(
                            psP[:, :], k.onesrfB[0:1, 0:128], prefrow[:, :],
                            start=True, stop=False))
                        cp1 = comp("tensor", lambda t: t.matmul(
                            psP[:, :], k.triliB[:, :], mskb16[:, :],
                            start=False, stop=True))
                        wait("vector", cp1)
                        comp("vector", lambda v: v.tensor_tensor(
                            out=posb[:, :], in0=psP[:, :], in1=mskf[:, :],
                            op=ALU.mult))
                        vf()
                        cp2 = comp("vector", lambda v, c=c: v.reduce_sum(
                            out=lpcols[:, c:c + 1], in_=posb[:, :], axis=AX.X))
                        wait("tensor", cp2)
                    raw("vector", lambda v, c=c: v.tensor_copy(
                        out=gcols[:, c:c + 1], in_=gcol[:, :]))
                    cp3 = comp("vector", lambda v, c=c: v.tensor_copy(
                        out=icols[:, c:c + 1], in_=idxf[:, :]))
                    wait("vector", cl3)
                    cp4 = comp("vector", lambda v: v.tensor_scalar(
                        out=poolr[:, :], in0=uln2[:, :], scalar1=gcol[:, :],
                        scalar2=None, op0=ALU.mult))
                    wait("sync", cp4)
                    tpl = dma("sync", k.pool[c * 128:(c + 1) * 128, :],
                              poolr[:, :], sem=k.sze)
                    wait("vector", tpl)
                cc1 = comp("tensor", lambda t, c=c: t.matmul(
                    psCnt[:, :], k.onescbB[:, :], mskb16[:, :],
                    start=True, stop=True))
                wait("vector", cc1)
                vf()
                cc2 = comp("vector", lambda v: v.tensor_tensor(
                    out=prefrow[:, :], in0=prefrow[:, :], in1=psCnt[:, :],
                    op=ALU.add))
                wait("tensor", cc2)

            vf()
            cf1 = comp("vector", lambda v: v.tensor_copy(
                out=ctmp[:, :], in_=prefrow[:, :]))
            wait("sync", cf1)
            tcf = dma("sync", k.cntsB[job:job + 1, :], ctmp[:, :],
                      sem=k.sze)
            wait("vector", tcf)
            tap("x1b", x1b[:, :])
            tap("lgs", lgs[:, :])
            tap("lpcols", lpcols[:, :])
            tap("gcols", gcols[:, :])
            tap("icols", icols[:, :])
        k.barrier()
        SD.close()


def rendezvous(k):
    nc, e = k.nc, k.ctx.enter_context
    wait, dma, comp, raw = k.wait, k.dma, k.comp, k.raw
    k.ctabB = e(_sb(nc, "ctabB", [NEXP, NEXP], F32))
    cz = comp("vector", lambda v: v.memset(k.ctabB[:, :], 0.0))
    k.rdvscr = e(_sb(nc, "rdvscr", [1, 16], I32))
    arow = e(_sb(nc, "arow", [2, 16], I32))
    brow = e(_sb(nc, "brow", [2, 16], I32))
    crow = e(_sb(nc, "crow", [2, 16], I32))
    frow = e(_sb(nc, "frow", [2, 16], I32))

    MAGICA = MAGIC
    MAGICB = MAGIC + 1
    c1 = comp("vector", lambda v: v.memset(arow[:, :], 0))
    k.vf()
    c2 = comp("vector", lambda v: v.memset(arow[:, 0:1], MAGICA))
    comp("vector", lambda v: v.memset(brow[:, :], 0))
    k.vf()
    comp("vector", lambda v: v.memset(brow[:, 0:1], MAGICA))
    k.vf()
    c3 = comp("vector", lambda v: v.memset(brow[:, 1:2], MAGICB))
    comp("vector", lambda v: v.memset(crow[:, :], 0))
    k.vf()
    c4 = comp("vector", lambda v: v.memset(crow[:, 1:2], MAGICB))
    wait("gpsimd", c1, c2, c3, c4, cz, (k.sze, k.sze.n))
    sgh = k.sg
    base = sgh.n

    bcf = nc.gpsimd.alloc_register("bcflag")
    bcc = nc.gpsimd.alloc_register("bccnt")

    def poll(g, nd, rc, f0, f1, f2, col, val):
        g.reg_mov(nd, 1)
        with g.While(nd):
            g.reg_add(rc, rc, 16)
            g.dma_start(out=frow[:, :],
                        in_=k.sh_flag[0:2, :]).then_inc(sgh.h, 16)
            g.wait_ge(sgh.h, rc)
            g.load(f0, frow[0:1, col:col + 1])
            g.load(f1, frow[1:2, col:col + 1])
            g.reg_alu(f0, f0, val, ALU.is_equal)
            g.reg_alu(f1, f1, val, ALU.is_equal)
            g.reg_alu(f2, f0, f1, ALU.bitwise_and)
            g.reg_alu(nd, f2, 1, ALU.is_lt)

    def body(g):
        g.reg_mov(bcf, 2)
        g.reg_mov(bcc, NEXP + 2)
        with (g.register("nd") as nd, g.register("rc") as rc,
              g.register("f0") as f0, g.register("f1") as f1,
              g.register("f2") as f2):
            g.reg_mov(rc, base)
            g.dma_start(out=k.sh_cnt[0:NEXP, :],
                        in_=k.zer88[:, :]).then_inc(sgh.h, 16)
            g.dma_start(out=k.sh_cnt[NEXP:NEXP + 3, :],
                        in_=k.zer88[0:3, :]).then_inc(sgh.h, 16)
            g.reg_add(rc, rc, 32)
            g.wait_ge(sgh.h, rc)
            g.indirect_dma_start(
                out=k.sh_flag,
                out_offset=IndirectOffsetOnAxis(k.parrB[:, 0:1], 0),
                in_=arow[:, :], in_offset=None,
                bounds_check=bcf, oob_is_err=False).then_inc(sgh.h, 16)
            g.reg_add(rc, rc, 16)
            g.wait_ge(sgh.h, rc)
            poll(g, nd, rc, f0, f1, f2, 0, MAGICA)
            g.indirect_dma_start(
                out=k.sh_cnt,
                out_offset=IndirectOffsetOnAxis(k.jobrB[:, 0:1], 0),
                in_=k.cntsB[:, :], in_offset=None,
                bounds_check=bcc, oob_is_err=False).then_inc(sgh.h, 16)
            g.reg_add(rc, rc, 16)
            g.wait_ge(sgh.h, rc)
            g.indirect_dma_start(
                out=k.sh_flag,
                out_offset=IndirectOffsetOnAxis(k.parrB[:, 0:1], 0),
                in_=brow[:, :], in_offset=None,
                bounds_check=bcf, oob_is_err=False).then_inc(sgh.h, 16)
            g.reg_add(rc, rc, 16)
            g.wait_ge(sgh.h, rc)
            poll(g, nd, rc, f0, f1, f2, 1, MAGICB)
            g.indirect_dma_start(
                out=k.ctabB[:, :], out_offset=None,
                in_=k.sh_cnt,
                in_offset=IndirectOffsetOnAxis(k.vrowsB[:, 0:1], 0),
                bounds_check=bcc, oob_is_err=False).then_inc(sgh.h, 16)
            g.reg_add(rc, rc, 16)
            g.wait_ge(sgh.h, rc)
            g.indirect_dma_start(
                out=k.sh_flag,
                out_offset=IndirectOffsetOnAxis(k.parrB[:, 0:1], 0),
                in_=crow[:, :], in_offset=None,
                bounds_check=bcf, oob_is_err=False).then_inc(sgh.h, 16)
            g.reg_add(rc, rc, 16)
            g.wait_ge(sgh.h, rc)
    if RDV_LOCAL:
        def body_local(g):
            g.dma_start(out=k.ctabB[0:NJOBS, :],
                        in_=k.cntsB[:, :]).then_inc(sgh.h, 16)
            g.wait_ge(sgh.h, base + 16)
        k.P.steps.append(("gpsimd", body_local))
    else:
        k.P.steps.append(("gpsimd", body))
    sgh.n = base + 128
    cq = comp("gpsimd", lambda g: g.memset(k.rdvscr[:, :], 0))
    for en in ("sync", "tensor", "vector", "scalar"):
        wait(en, cq)


def ffn(k):
    nc, e = k.nc, k.ctx.enter_context
    wait, dma, comp, raw, tap = k.wait, k.dma, k.comp, k.raw, k.tap
    vf = k.vf

    bcregs = {}
    for bval in (7, NTIL * 128 - 1, NEXP * DM - 1):
        bcregs[bval] = k.nc.gpsimd.alloc_register(f"bc{bval}")

    def _initbc(g, bcregs=bcregs):
        for bval, reg in bcregs.items():
            g.reg_mov(reg, bval)
    k.P.steps.append(("gpsimd", _initbc))

    def idma(out, out_off, in_, in_off, bc):
        s = k.sg2
        s.n += 16
        reg = bcregs[bc]

        def fn(g, out=out, out_off=out_off, in_=in_, in_off=in_off, reg=reg):
            g.indirect_dma_start(
                out=out, out_offset=out_off, in_=in_, in_offset=in_off,
                bounds_check=reg, oob_is_err=False).then_inc(s.h, 16)
        k.P.steps.append(("gpsimd", fn))
        return (s, s.n)

    with ExitStack() as S:
        es = S.enter_context
        offsr = es(_sb(nc, "offsr", [1, NEXP], F32))
        mycnt = es(_sb(nc, "mycnt", [1, NEXP], F32))
        pcr = es(_sb(nc, "pcr", [1, NEXP], F32))
        mycntc = es(_sb(nc, "mycntc", [NEXP, 1], F32))
        pcc = es(_sb(nc, "pcc", [NEXP, 1], F32))
        cmpP = es(_sb(nc, "cmpP", [NEXP, NTIL], F32))
        inclr = es(_sb(nc, "inclr", [1, NEXP], F32))
        plocr = es(_sb(nc, "plocr", [1, NEXP], F32))
        zer8 = es(_sb(nc, "zer8", [1, NEXP], F32))
        inclc = es(_sb(nc, "inclc", [NEXP, 1], F32))
        cmp8 = es(_sb(nc, "cmp8", [NEXP, NTIL], F32))
        cmpb = es(_sb(nc, "cmpb", [NEXP, NTIL], BF16))
        etbl = es(_sb(nc, "etbl", [1, NTIL], F32))
        etblp = es(_sb(nc, "etblp", [1, NTIL + 1], F32))
        etadj = es(_sb(nc, "etadj", [1, NTIL], F32))
        etadjB = es(_sb(nc, "etadjB", [128, NTIL], F32))
        ebadj = es(_sb(nc, "ebadj", [1, NTIL], F32))
        flg = es(_sb(nc, "flg", [1, NTIL], F32))
        plocB = es(_sb(nc, "plocB", [128, NEXP], F32))
        offsB = es(_sb(nc, "offsB", [128, NEXP], F32))
        mskf2 = es(_sb(nc, "mskf2", [128, NEXP], F32))
        tcol = es(_sb(nc, "tcol", [128, 1], F32))
        tcol2 = es(_sb(nc, "tcol2", [128, 1], F32))
        dcols = es(_sb(nc, "dcols", [128, NCH], I32))
        gkcols = es(_sb(nc, "gkcols", [128, NCH], F32))
        poolc = es(_sb(nc, "poolc", [128, DM], BF16))
        bidx = es(_sb(nc, "bidx", [2, 1], I32))
        wi = es(_sb(nc, "wi", [128, 1], I32))
        w1S = es(_sb(nc, "w1S", [128, 8 * DM], BF16))
        w2S = es(_sb(nc, "w2S", [128, 8 * DM], BF16))
        bd1h = es(_sb(nc, "bd1h", [2, DM], BF16))
        bd1l = es(_sb(nc, "bd1l", [2, DM], BF16))
        bd2h = es(_sb(nc, "bd2h", [2, DM], BF16))
        bd2l = es(_sb(nc, "bd2l", [2, DM], BF16))
        srows = es(_sb(nc, "srows", [128, DM], BF16))
        xsT = es(_sb(nc, "xsT", [128, 8 * 128], BF16))
        hfm = es(_sb(nc, "hfm", [128, 8 * 128], BF16))
        eot = es(_sb(nc, "eot", [128, DM], F32))
        eog = es(_sb(nc, "eog", [128, DM], F32))
        x1r = es(_sb(nc, "x1r", [128, DM], F32))
        outr = es(_sb(nc, "outr", [128, DM], F32))

        with _ps(nc, "psF", [128, 64], F32) as psF:
            c0 = comp("vector", lambda v: v.memset(zer8[:, :], 0.0))
            cmy = comp("vector", lambda v: v.tensor_copy(
                out=mycnt[:, :], in_=k.cntsB[0:1, :]))
            wait("vector", (k.sq, k.sq.n))
            csel = comp("vector", lambda v: v.select(
                out=k.ctabB[:, :], mask=k.validB[:, :],
                on_true=k.ctabB[:, :], on_false=k.zer88[:, :]))
            wait("tensor", (k.sv, k.sv.n), (k.sq, k.sq.n), csel)
            cf1 = comp("tensor", lambda t: t.matmul(
                psF[0:1, 0:NEXP], k.prefmB[:, :], k.ctabB[:, :],
                start=True, stop=True))
            wait("vector", cf1)
            cf2 = comp("vector", lambda v: v.tensor_copy(
                out=offsr[:, :], in_=psF[0:1, 0:NEXP]))
            wait("tensor", cf2)
            wait("sync", cmy)
            tmc = dma("sync", mycntc[:, :], mycnt[0:1, :], sem=k.sze,
                      allow_slow_non_contiguous=True)
            wait("vector", tmc)
            comp("vector", lambda v: v.tensor_scalar(
                out=cmpP[:, :], in0=k.tilestB[:, :],
                scalar1=mycntc[:, 0:1], scalar2=None, op0=ALU.is_lt))
            vf()
            comp("vector", lambda v: v.reduce_sum(
                out=pcc[:, :], in_=cmpP[:, :], axis=AX.X))
            vf()
            cC = comp("vector", lambda v: v.tensor_scalar_mul(
                pcc[:, :], pcc[:, :], 128.0))
            wait("sync", cC)
            tpr = dma("sync", pcr[0:1, :], pcc[:, :], sem=k.sze,
                      allow_slow_non_contiguous=True)
            wait("vector", tpr, c0)
            c3 = comp("vector", lambda v: v.tensor_tensor_scan(
                out=inclr[:, :], data0=pcr[:, :], data1=zer8[:, :],
                initial=0.0, op0=ALU.add, op1=ALU.add))
            wait("vector", c3)
            c4 = comp("vector", lambda v: v.tensor_tensor(
                out=plocr[:, :], in0=inclr[:, :], in1=pcr[:, :],
                op=ALU.subtract))
            wait("sync", c3)
            t1 = dma("sync", inclc[:, :], inclr[:, :], sem=k.sze)
            wait("vector", t1)
            c5 = comp("vector", lambda v: v.tensor_scalar(
                out=cmp8[:, :], in0=k.tilestB[:, :], scalar1=inclc[:, :],
                scalar2=None, op0=ALU.is_ge))
            wait("vector", c5)
            c6 = comp("vector", lambda v: v.tensor_copy(out=cmpb[:, :],
                                                        in_=cmp8[:, :]))
            wait("tensor", c6)
            cf3 = comp("tensor", lambda t: t.matmul(
                psF[0:1, 32:32 + NTIL], k.onescbB[0:NEXP, :], cmpb[:, :],
                start=True, stop=True))
            wait("vector", cf3)
            c7 = comp("vector", lambda v: v.tensor_copy(
                out=etbl[:, :], in_=psF[0:1, 32:32 + NTIL]))
            wait("vector", c7)
            raw("vector", lambda v: v.memset(etblp[:, 0:1], -1.0))
            c8 = comp("vector", lambda v: v.tensor_copy(
                out=etblp[:, 1:NTIL + 1], in_=etbl[:, :]))
            wait("vector", c8)
            c9 = comp("vector", lambda v: v.tensor_tensor(
                out=flg[:, :], in0=etbl[:, :], in1=etblp[:, 0:NTIL],
                op=ALU.is_equal))
            wait("vector", c9)
            raw("vector", lambda v: v.tensor_scalar(
                out=etadj[:, :], in0=flg[:, :], scalar1=float(1 << 27),
                scalar2=None, op0=ALU.mult))
            comp("vector", lambda v: v.tensor_scalar(
                out=ebadj[:, :], in0=etbl[:, :], scalar1=1024.0,
                scalar2=None, op0=ALU.mult))
            vf()
            c11 = comp("vector", lambda v: v.tensor_tensor(
                out=etadj[:, :], in0=etadj[:, :], in1=ebadj[:, :],
                op=ALU.add))
            c12 = comp("vector", lambda v: v.tensor_copy(out=ebadj[:, :],
                                                         in_=etbl[:, :]))
            wait("tensor", c4, cf2, c11, c12)
            cf4 = comp("tensor", lambda t: t.matmul(
                psF[:, 32:32 + NEXP], k.onesrfB[0:1, 0:128], plocr[:, :],
                start=True, stop=True))
            wait("vector", cf4)
            cb1 = comp("vector", lambda v: v.tensor_copy(
                out=plocB[:, :], in_=psF[:, 32:32 + NEXP]))
            wait("tensor", cb1)
            cf5 = comp("tensor", lambda t: t.matmul(
                psF[:, 32:32 + NEXP], k.onesrfB[0:1, 0:128], offsr[:, :],
                start=True, stop=True))
            wait("vector", cf5)
            cb2 = comp("vector", lambda v: v.tensor_copy(
                out=offsB[:, :], in_=psF[:, 32:32 + NEXP]))
            wait("tensor", cb2)
            cf6 = comp("tensor", lambda t: t.matmul(
                psF[:, 32:32 + NTIL], k.onesrfB[0:1, 0:128], etadj[:, :],
                start=True, stop=True))
            wait("vector", cf6)
            cb3 = comp("vector", lambda v: v.tensor_copy(
                out=etadjB[:, :], in_=psF[:, 32:32 + NTIL]))
            wait("tensor", cb3)

        tap("etbl", etbl[:, :])
        tap("plocr", plocr[:, :])
        tap("offsr", offsr[:, :])
        tap("ctab", k.ctabB[:, :])
        tap("mycnt", mycnt[:, :])

        pscr = es(_sb(nc, "pscr", [128, NEXP], F32))
        wic = es(_sb(nc, "wic", [128, 8], I32))
        wicf = es(_sb(nc, "wicf", [128, 8], F32))
        for c in range(NCH):
            cq1 = comp("vector", lambda v, c=c: v.tensor_scalar(
                out=mskf2[:, :], in0=k.iota8B[:, :],
                scalar1=k.icols[:, c:c + 1], scalar2=None, op0=ALU.is_equal))
            wait("vector", cq1)
            comp("vector", lambda v: v.tensor_tensor(
                out=pscr[:, :], in0=mskf2[:, :], in1=plocB[:, :],
                op=ALU.mult))
            vf()
            comp("vector", lambda v: v.reduce_sum(
                out=tcol[:, :], in_=pscr[:, :], axis=AX.X))
            vf()
            comp("vector", lambda v, c=c: v.tensor_scalar(
                out=tcol[:, :], in0=tcol[:, :],
                scalar1=k.lpcols[:, c:c + 1], scalar2=-1.0,
                op0=ALU.add, op1=ALU.add))
            vf()
            cq2 = comp("vector", lambda v, c=c: v.tensor_copy(
                out=dcols[:, c:c + 1], in_=tcol[:, :]))
            vf()
            comp("vector", lambda v: v.tensor_tensor(
                out=pscr[:, :], in0=mskf2[:, :], in1=offsB[:, :],
                op=ALU.mult))
            vf()
            comp("vector", lambda v: v.reduce_sum(
                out=tcol2[:, :], in_=pscr[:, :], axis=AX.X))
            vf()
            comp("vector", lambda v, c=c: v.tensor_scalar(
                out=tcol2[:, :], in0=tcol2[:, :],
                scalar1=k.lpcols[:, c:c + 1], scalar2=None, op0=ALU.add))
            vf()
            comp("vector", lambda v: v.tensor_scalar(
                out=tcol2[:, :], in0=tcol2[:, :], scalar1=float(CAP),
                scalar2=None, op0=ALU.is_lt))
            vf()
            cq3 = comp("vector", lambda v, c=c: v.tensor_scalar(
                out=gkcols[:, c:c + 1], in0=tcol2[:, :],
                scalar1=k.gcols[:, c:c + 1], scalar2=None, op0=ALU.mult))
            tp1 = dma("sync", poolc[:, :], k.pool[c * 128:(c + 1) * 128, :],
                      sem=k.sze)
            wait("gpsimd", cq2, cq3, tp1)
            ts1 = idma(k.sorted_x, IndirectOffsetOnAxis(dcols[:, c:c + 1], 0),
                       poolc[:, :], None, NTIL * 128 - 1)
            wait("sync", ts1)

        tap("dcols", dcols[:, :])
        tap("gkcols", gkcols[:, :])

        cbi = comp("vector", lambda v: v.memset(bidx[:, :], 100))
        with _ps(nc, "psH", [128, 8 * 128], F32) as psH, \
             _ps(nc, "psO2", [128, DM], F32) as psO2, \
             _ps(nc, "ptpH", [128, 512], BF16) as ptpH:
            for j in range(NTIL):
                wait("vector", (k.sg2, k.sg2.n), (k.sp, k.sp.n))
                comp("vector", lambda v, j=j: v.tensor_single_scalar(
                    out=wicf[:, :], in_=k.iotakB[:, :],
                    scalar=etadjB[:, j:j + 1], op=ALU.add))
                vf()
                raw("vector", lambda v: v.tensor_copy(
                    out=wic[:, :], in_=wicf[:, :]))
                cw1 = comp("vector", lambda v, j=j: v.tensor_copy(
                    out=bidx[0:1, 0:1], in_=ebadj[0:1, j:j + 1]))
                wait("gpsimd", cw1, (k.sp, k.sp.n))
                gw = []
                for kc in range(8):
                    gw.append(idma(w1S[:, kc * DM:(kc + 1) * DM], None,
                                   k.w1bf,
                                   IndirectOffsetOnAxis(wic[:, kc:kc + 1], 0),
                                   NEXP * DM - 1))
                    gw.append(idma(w2S[:, kc * DM:(kc + 1) * DM], None,
                                   k.w2bf,
                                   IndirectOffsetOnAxis(wic[:, kc:kc + 1], 0),
                                   NEXP * DM - 1))
                gw.append(idma(bd1h[:, :], None, k.b1hi,
                               IndirectOffsetOnAxis(bidx[:, 0:1], 0), 7))
                gw.append(idma(bd1l[:, :], None, k.b1lo,
                               IndirectOffsetOnAxis(bidx[:, 0:1], 0), 7))
                gw.append(idma(bd2h[:, :], None, k.b2hi,
                               IndirectOffsetOnAxis(bidx[:, 0:1], 0), 7))
                gw.append(idma(bd2l[:, :], None, k.b2lo,
                               IndirectOffsetOnAxis(bidx[:, 0:1], 0), 7))
                wait("sync", (k.sp, k.sp.n), (k.sg2, k.sg2.n))
                tr1 = dma("sync", srows[:, :],
                          k.sorted_x[j * 128:(j + 1) * 128, :], sem=k.sze)
                wait("tensor", tr1)
                for g4 in range(2):
                    for k2 in range(4):
                        em = (lambda t, g4=g4, k2=k2: t.transpose(
                            ptpH[:, k2 * 128:(k2 + 1) * 128],
                            srows[:, (g4 * 4 + k2) * 128:
                                  (g4 * 4 + k2 + 1) * 128],
                            k.identB[:, :]))
                        if k2 == 3:
                            cs1 = comp("tensor", em)
                        else:
                            raw("tensor", em)
                    wait("scalar", cs1)
                    cs2 = comp("scalar", lambda s, g4=g4: s.copy(
                        out=xsT[:, g4 * 512:(g4 + 1) * 512],
                        in_=ptpH[:, 0:512]))
                    wait("tensor", cs2)
                wait("tensor", *gw)
                for ft in range(8):
                    for kc in range(8):
                        raw("tensor", lambda t, ft=ft, kc=kc: t.matmul(
                            psH[:, ft * 128:(ft + 1) * 128],
                            w1S[:, kc * DM + ft * 128:kc * DM + (ft + 1) * 128],
                            xsT[:, kc * 128:(kc + 1) * 128],
                            start=(kc == 0), stop=False))
                    raw("tensor", lambda t, ft=ft: t.matmul(
                        psH[:, ft * 128:(ft + 1) * 128],
                        bd1h[0:1, ft * 128:(ft + 1) * 128],
                        k.onesrbB[:, :], start=False, stop=False))
                    em = (lambda t, ft=ft: t.matmul(
                        psH[:, ft * 128:(ft + 1) * 128],
                        bd1l[0:1, ft * 128:(ft + 1) * 128],
                        k.onesrbB[:, :], start=False, stop=True))
                    if ft == 7:
                        cm1 = comp("tensor", em)
                    else:
                        raw("tensor", em)
                wait("scalar", cm1)
                cm2 = comp("scalar", lambda s: s.activation(
                    out=hfm[:, :], in_=psH[:, :], func=AF.Relu))
                wait("tensor", cm2)
                for kf in range(8):
                    for nb in (0, 1):
                        raw("tensor", lambda t, kf=kf, nb=nb: t.matmul(
                            psO2[:, nb * 512:(nb + 1) * 512],
                            hfm[:, kf * 128:(kf + 1) * 128],
                            w2S[:, kf * DM + nb * 512:
                                kf * DM + (nb + 1) * 512],
                            start=(kf == 0), stop=False))
                for nb in (0, 1):
                    raw("tensor", lambda t, nb=nb: t.matmul(
                        psO2[:, nb * 512:(nb + 1) * 512], k.onesrbB[:, :],
                        bd2h[0:1, nb * 512:(nb + 1) * 512],
                        start=False, stop=False))
                cm3 = None
                for nb in (0, 1):
                    cm3 = comp("tensor", lambda t, nb=nb: t.matmul(
                        psO2[:, nb * 512:(nb + 1) * 512], k.onesrbB[:, :],
                        bd2l[0:1, nb * 512:(nb + 1) * 512],
                        start=False, stop=True))
                wait("vector", cm3)
                cm4 = comp("vector", lambda v: v.tensor_copy(
                    out=eot[:, :], in_=psO2[:, :]))
                wait("tensor", cm4)
                wait("sync", cm4)
                te1 = dma("sync", k.eo_sp[j * 128:(j + 1) * 128, :],
                          eot[:, :], sem=k.sze)
                wait("vector", te1)

        for c in range(NCH):
            wait("gpsimd", (k.sv, k.sv.n), (k.sze, k.sze.n))
            tg1 = idma(eog[:, :], None, k.eo_sp,
                       IndirectOffsetOnAxis(dcols[:, c:c + 1], 0),
                       NTIL * 128 - 1)
            tx1 = dma("sync", x1r[:, :], k.x1_sp[c * 128:(c + 1) * 128, :],
                      sem=k.sze)
            wait("vector", tg1, tx1)
            co1 = comp("vector", lambda v, c=c: v.scalar_tensor_tensor(
                out=outr[:, :], in0=eog[:, :],
                scalar=gkcols[:, c:c + 1], in1=x1r[:, :],
                op0=ALU.mult, op1=ALU.add))
            wait("sync", co1)
            to1 = dma("sync", k.x2_out[c * 128:(c + 1) * 128, :], outr[:, :],
                      sem=k.sze)
            wait("sync", to1)
            wait("vector", to1)
        k.barrier()


def build_full():
    k = _mk()
    prep(k)
    if STOP_AFTER != "prep":
        for job in range(EMIT_JOBS):
            attn_job(k, job)
        if STOP_AFTER is None:
            rendezvous(k)
            if EMIT_FFN:
                ffn(k)
    k.barrier()
    with k.ctx:
        with k.nc.Block() as blk:
            k.P.build(blk)
    return k


def host_inputs(inputs, core):
    f32 = np.float32
    u = np.ascontiguousarray(np.asarray(inputs["L0_attn_u"], f32))
    jobs = [core - 2 * i for i in range(NJOBS)]
    jobs = [b if b >= 0 else jobs[0] for b in jobs]
    seen = set()
    rows = np.zeros((NJOBS, 1), np.int32)
    for s, b in enumerate(jobs):
        if b in seen:
            rows[s] = NEXP + (s - 1)
        else:
            rows[s] = b
            seen.add(b)
    bf = ml_dtypes.bfloat16
    ident = np.eye(128, dtype=bf)
    sidx = np.arange(128)
    maskf = (sidx[:, None] < sidx[None, :]).astype(bf)
    maskb = (sidx[:, None] > sidx[None, :]).astype(bf)
    trili = (sidx[:, None] <= sidx[None, :]).astype(bf)
    sel3 = np.zeros((96, 4096), bf)
    for r in range(96):
        h = r % 32
        sel3[r, h * 128:(h + 1) * 128] = 1
    sel2h4 = np.zeros((4, 128), bf)
    sel2h4[0, 0:64] = 1
    sel2h4[1, 64:128] = 1
    sel2h4[2, 0:64] = 1
    sel2h4[3, 64:128] = 1
    conv_w = np.asarray(inputs["L0_attn_conv_w"], f32)
    conv_wc = np.zeros((128, 126), f32)
    for j in range(18):
        for t in range(7):
            conv_wc[:, j * 7 + t] = conv_w[j * 128:(j + 1) * 128, 0, t]
    conv_bc = np.asarray(inputs["L0_attn_conv_b"], f32).reshape(18, 128).T.copy()
    dtb = np.asarray(inputs["L0_attn_dt_bias"], f32)
    alog = np.asarray(inputs["L0_attn_A_log"], f32)
    W1 = np.asarray(inputs["L0_ffn_W1"], f32)
    W2 = np.asarray(inputs["L0_ffn_W2"], f32)
    return {
        "u_jobs": np.concatenate([u[b] for b in jobs], 0),
        "wt_in": np.ascontiguousarray(
            np.asarray(inputs["L0_attn_in_proj_w"], f32).T),
        "wt_out": np.ascontiguousarray(
            np.asarray(inputs["L0_attn_out_proj_w"], f32).T),
        "fcdt": np.ascontiguousarray(np.asarray(inputs["L0_attn_fc_D_w"], f32).T),
        "conv_wc": conv_wc,
        "conv_bc": np.ascontiguousarray(conv_bc),
        "dtb2": np.concatenate([dtb, dtb]).reshape(64, 1),
        "alog2": np.concatenate([alog, alog]).reshape(64, 1),
        "dparr": np.asarray(inputs["L0_attn_D_param"], f32).reshape(1, NH),
        "nwc": np.asarray(inputs["L0_attn_norm_w"], f32).reshape(16, 128).T.copy(),
        "wr": np.asarray(inputs["L0_ffn_Wr"], f32),
        "br": np.asarray(inputs["L0_ffn_br"], f32).reshape(1, NEXP),
        "w1f": W1.reshape(NEXP * DM, DM),
        "w2f": W2.reshape(NEXP * DM, DM),
        "b1r": np.asarray(inputs["L0_ffn_b1"], f32),
        "b2r": np.asarray(inputs["L0_ffn_b2"], f32),
        "identb": ident,
        "identf": np.eye(128, dtype=f32),
        "maskf": maskf,
        "maskb": maskb,
        "trili": trili,
        "sel3": sel3,
        "sel2h4": sel2h4,
        "iota8r": np.broadcast_to(np.arange(8, dtype=f32), (128, 8)).copy(),
        "iotak": (np.arange(128)[:, None] +
                  128 * np.arange(8)[None, :]).astype(f32),
        "tilest": np.broadcast_to(128.0 * np.arange(NTIL, dtype=f32),
                                  (NEXP, NTIL)).copy(),
        "onescb": np.ones((128, 1), bf),
        "onesrb": np.ones((1, 128), bf),
        "ones2b": np.ones((2, 128), bf),
        "onesrf": np.ones((1, 512), f32),
        "prefm": (np.arange(8) < core).astype(f32).reshape(NEXP, 1),
        "validm": np.broadcast_to(
            (np.arange(8) < 2 * (core // 2) + 2).astype(np.int32)[:, None],
            (NEXP, NEXP)).copy(),
        "jobrows": rows,
        "parrow": np.array([[core % 2], [2]], np.int32),
        "sorted_x": np.zeros((NTIL * 128, DM), bf),
        "vrows": np.where(np.arange(8) < 2 * (core // 2) + 2,
                          np.arange(8), 1000).astype(np.int32).reshape(8, 1),
        "cvals": np.broadcast_to(
            np.array([1e-5, 1.0, -1.0, -0.5, 1.0 / 2048.0, 0.0, 0.0, 0.0],
                     f32), (128, 8)).copy(),
    }


def kernel(**inputs):
    k = build_full()
    in_maps = [host_inputs(inputs, c) for c in range(8)]
    res = run_bass_kernel_spmd(k.nc, in_maps, core_ids=list(range(8)))
    out = np.stack([np.asarray(res.results[c]["x2"]) for c in range(8)], 0)
    return out.astype(np.float32)
